# version 01 (final)
# baseline (speedup 1.0000x reference)
import numpy as np
import ml_dtypes
from contextlib import ExitStack
import concourse.bass as bass
import concourse.mybir as mybir
from concourse.bass_utils import run_bass_kernel_spmd

F32 = mybir.dt.float32
BF16 = mybir.dt.bfloat16
AF = mybir.ActivationFunctionType
ALU = mybir.AluOpType
BF = ml_dtypes.bfloat16

NCORE = 8
D = 1024
NCH = D // 128
TV = 2048
TL = 2052
SPANS = [(0, 512), (512, 1024), (1024, 1536), (1536, 2048), (1540, 2052)]


class Ctr:
    def __init__(self, sem):
        self.sem = sem
        self.n = 0

    def inc(self, inst, k=1):
        inst.then_inc(self.sem, k)
        self.n += k
        return self.n

    def wait(self, eng, val=None):
        eng.wait_ge(self.sem, self.n if val is None else val)


def new_nc():
    return bass.Bass("TRN2", target_bir_lowering=False, detect_race_conditions=False)


def bfc(x):
    return np.ascontiguousarray(np.asarray(x, BF))


def build_preA(dbg_stage=None):
    nc = new_nc()
    xh = nc.dram_tensor("xh", [D, TL], F32, kind="ExternalInput").ap()
    w_in = nc.dram_tensor("w_in", [D, 2048], F32, kind="ExternalInput").ap()
    cw = nc.dram_tensor("cw", [128, 32], F32, kind="ExternalInput").ap()
    cb = nc.dram_tensor("cb", [128, 8], F32, kind="ExternalInput").ap()
    zw = nc.dram_tensor("zw", [128, 32], F32, kind="ExternalInput").ap()
    zb = nc.dram_tensor("zb", [128, 8], F32, kind="ExternalInput").ap()
    onesm = nc.dram_tensor("onesm", [128, 128], BF16, kind="ExternalInput").ap()
    xs_o = nc.dram_tensor("xs_o", [D, TV], BF16, kind="ExternalOutput").ap()
    zs_o = nc.dram_tensor("zs_o", [D, TV], BF16, kind="ExternalOutput").ap()
    dbg_o = nc.dram_tensor("dbg_o", [128, 4 * TL], F32, kind="ExternalOutput").ap()

    es = ExitStack()
    sb = lambda name, shape, dt: es.enter_context(nc.sbuf_tensor(name, shape, dt))
    ps = lambda name, shape: es.enter_context(nc.psum_tensor(name, shape, F32))

    Xb = sb("Xb", [128, NCH * TL], BF16)
    LnB = sb("LnB", [128, NCH * TL], BF16)
    XZ = sb("XZ", [128, 2 * TL], BF16)
    outb = sb("outb", [128, 2 * TV], BF16)
    Wi = sb("Wi", [128, NCH * 2048], BF16)
    stg = sb("stg", [128, TL], F32)
    wstg = sb("wstg", [128, 2048], F32)
    sq = sb("sq", [128, 2 * 512], BF16)
    mrow = sb("mrow", [128, TL], F32)
    rrow = sb("rrow", [128, TL], F32)
    cvw = sb("cvw", [128, 64], F32)
    cvb = sb("cvb", [128, 16], F32)
    som = sb("som", [128, 128], BF16)
    ctmp = sb("ctmp", [128, 2 * TV], BF16)
    var_s = sb("var_s", [128, TL], F32)
    p2s = sb("p2s", [128, 512], F32)
    eps5 = sb("eps5", [128, 1], F32)

    P1 = [ps("P1a", [128, 512]), ps("P1b", [128, 512])]
    P2 = [ps("P2a", [128, 512]), ps("P2b", [128, 512])]
    PZ = [ps(f"PZ{i}", [128, 512]) for i in range(4)]

    c_ld = Ctr(es.enter_context(nc.semaphore("s_ld")))
    c_ldc = Ctr(es.enter_context(nc.semaphore("s_ldc")))
    c_st0 = Ctr(es.enter_context(nc.semaphore("s_st0")))
    c_st1 = Ctr(es.enter_context(nc.semaphore("s_st1")))
    SLOT = [c_st0, c_st1]
    c_cast = Ctr(es.enter_context(nc.semaphore("s_cast")))
    c_sq = Ctr(es.enter_context(nc.semaphore("s_sq")))
    c_pe = Ctr(es.enter_context(nc.semaphore("s_pe")))
    c_dve = Ctr(es.enter_context(nc.semaphore("s_dve")))
    c_act = Ctr(es.enter_context(nc.semaphore("s_act")))
    c_st = Ctr(es.enter_context(nc.semaphore("s_st")))
    c_ev = Ctr(es.enter_context(nc.semaphore("s_ev")))
    c_cv = Ctr(es.enter_context(nc.semaphore("s_cv")))

    with es:
        with nc.Block() as b:
            ld_marks = {}
            cast_marks = {}
            for f in range(NCH):
                ld_marks[("x", f)] = 16 * (f + 1)
                ld_marks[("w", f)] = 16 * (NCH + f + 1)
                cast_marks[("x", f)] = f + 1
                cast_marks[("w", f)] = NCH + f + 1

            @b.sync
            def _(e):
                c_ldc.inc(e.dma_start(out=cvw[:, 0:32], in_=cw), 16)
                c_ldc.inc(e.dma_start(out=cvw[:, 32:64], in_=zw), 16)
                c_ldc.inc(e.dma_start(out=cvb[:, 0:8], in_=cb), 16)
                c_ldc.inc(e.dma_start(out=cvb[:, 8:16], in_=zb), 16)
                c_ldc.inc(e.dma_start(out=som[:], in_=onesm), 16)
                for f in range(NCH):
                    if f >= 1:
                        e.wait_ge(c_cast.sem, cast_marks[("x", f - 1)])
                    c_ld.inc(e.dma_start(out=stg[:, :],
                                         in_=xh[128 * f:128 * (f + 1), :]), 16)
                for f in range(NCH):
                    e.wait_ge(c_cast.sem, cast_marks[("w", f - 1)] if f >= 1 else cast_marks[("x", NCH - 1)])
                    c_ld.inc(e.dma_start(out=wstg[:, :],
                                         in_=w_in[128 * f:128 * (f + 1), :]), 16)

            @b.gpsimd
            def _(e):
                e.memset(eps5[:], 1e-5)

            @b.vector
            def _(e):
                for f in range(NCH):
                    e.wait_ge(c_ld.sem, ld_marks[("x", f)])
                    c_cast.inc(e.tensor_copy(Xb[:, f * TL:(f + 1) * TL], stg[:, :]))
                for f in range(NCH):
                    e.wait_ge(c_ld.sem, ld_marks[("w", f)])
                    c_cast.inc(e.tensor_copy(Wi[:, f * 2048:(f + 1) * 2048], wstg[:, :]))

        with nc.Block() as b:
            nsp = len(SPANS)
            sq_marks = [0] * (nsp * NCH)
            pe_marks = [0] * nsp
            dve_marks = [0] * nsp
            for k in range(nsp * NCH):
                sq_marks[k] = c_sq.n + k + 1
            for si in range(nsp):
                pe_marks[si] = c_pe.n + (si + 1) * NCH
                dve_marks[si] = c_dve.n + si + 1

            pe0_b2 = c_pe.n

            @b.scalar
            def _(e):
                for si, (lo, hi) in enumerate(SPANS):
                    for f in range(NCH):
                        k = si * NCH + f
                        if k >= 2:
                            e.wait_ge(c_pe.sem, pe0_b2 + k - 1)
                        c_sq.inc(e.activation(sq[:, (k % 2) * 512:(k % 2) * 512 + (hi - lo)],
                                              Xb[:, f * TL + lo:f * TL + hi], AF.Square))
                for si, (lo, hi) in enumerate(SPANS):
                    w = hi - lo
                    e.wait_ge(c_dve.sem, dve_marks[si])
                    e.activation(rrow[:, lo:hi], var_s[:, lo:hi], AF.Ln, bias=eps5[:, 0:1], scale=1.0)
                    c_act.inc(e.activation(rrow[:, lo:hi], rrow[:, lo:hi], AF.Exp, scale=-0.5))

            @b.tensor
            def _(e):
                e.wait_ge(c_ldc.sem, 16 * 5)
                for si, (lo, hi) in enumerate(SPANS):
                    w = hi - lo
                    if si >= 2:
                        e.wait_ge(c_dve.sem, dve_marks[si - 2])
                    for f in range(NCH):
                        e.matmul(P1[si % 2][:, 0:w], som[:], Xb[:, f * TL + lo:f * TL + hi],
                                 start=(f == 0), stop=(f == NCH - 1))
                    for f in range(NCH):
                        k = si * NCH + f
                        e.wait_ge(c_sq.sem, sq_marks[k])
                        c_pe.inc(e.matmul(P2[si % 2][:, 0:w], som[:],
                                          sq[:, (k % 2) * 512:(k % 2) * 512 + w],
                                          start=(f == 0), stop=(f == NCH - 1)))

            @b.vector
            def _(e):
                inv = 1.0 / 1024.0
                for si, (lo, hi) in enumerate(SPANS):
                    w = hi - lo
                    e.wait_ge(c_pe.sem, pe_marks[si])
                    e.tensor_scalar(mrow[:, lo:hi], P1[si % 2][:, 0:w], inv, None, ALU.mult)
                    e.tensor_scalar(p2s[:, 0:w], P2[si % 2][:, 0:w], inv, None, ALU.mult)
                    e.tensor_tensor(var_s[:, lo:hi], mrow[:, lo:hi], mrow[:, lo:hi], ALU.mult)
                    c_dve.inc(e.tensor_tensor(var_s[:, lo:hi], p2s[:, 0:w], var_s[:, lo:hi], ALU.subtract))

        if dbg_stage == "stats":
            with nc.Block() as b:
                @b.gpsimd
                def _(e):
                    c_st.inc(e.dma_start(out=dbg_o[:, 0:TL], in_=mrow[:]), 16)
                    c_st.inc(e.dma_start(out=dbg_o[:, TL:2 * TL], in_=rrow[:]), 16)
                    c_st.inc(e.dma_start(out=dbg_o[:, 2 * TL:3 * TL], in_=var_s[:]), 16)
                    e.wait_ge(c_st.sem, c_st.n)
            es.pop_all()
            return nc

        with nc.Block() as b:
            @b.vector
            def _(e):
                for f in range(NCH):
                    e.tensor_tensor(LnB[:, f * TL:(f + 1) * TL], Xb[:, f * TL:(f + 1) * TL], mrow[:], ALU.subtract)
                    e.tensor_tensor(LnB[:, f * TL:(f + 1) * TL], LnB[:, f * TL:(f + 1) * TL], rrow[:], ALU.mult)

        if dbg_stage == "apply":
            with nc.Block() as b:
                @b.gpsimd
                def _(e):
                    for f in range(4):
                        c_st.inc(e.dma_start(out=dbg_o[:, f * TL:(f + 1) * TL].bitcast(BF16)[:, 0:TL], in_=LnB[:, f * TL:(f + 1) * TL]), 16)
                    e.wait_ge(c_st.sem, c_st.n)
            es.pop_all()
            return nc

        with nc.Block() as b:
            pe0 = c_pe.n
            ev0 = c_ev.n
            cv0 = c_cv.n
            act0 = c_act.n
            st0 = c_st.n

            @b.tensor
            def _(e):
                k = 0
                for m in range(16):
                    for si, (lo, hi) in enumerate(SPANS):
                        w = hi - lo
                        if k >= 4:
                            e.wait_ge(c_ev.sem, ev0 + k - 3)
                        for f in range(NCH):
                            ins = e.matmul(PZ[k % 4][:, 0:w],
                                           Wi[:, f * 2048 + m * 128:f * 2048 + m * 128 + 128],
                                           LnB[:, f * TL + lo:f * TL + hi],
                                           start=(f == 0), stop=(f == NCH - 1))
                        c_pe.inc(ins)
                        k += 1

            @b.vector
            def _(e):
                e.wait_ge(c_ldc.sem, 16 * 5)
                k = 0
                for m in range(16):
                    slot = XZ[:, (m % 2) * TL:(m % 2 + 1) * TL]
                    for si, (lo, hi) in enumerate(SPANS):
                        w = hi - lo
                        e.wait_ge(c_pe.sem, pe0 + k + 1)
                        c_ev.inc(e.tensor_copy(slot[:, lo:hi], PZ[k % 4][:, 0:w]))
                        k += 1
                    isz = m >= 8
                    f = m - 8 if isz else m
                    wof = 32 * isz + f * 4
                    bof = 8 * isz + f
                    t0 = ctmp[:, 0:TV]
                    t1 = ctmp[:, TV:2 * TV]
                    if m >= 1:
                        e.wait_ge(c_act.sem, act0 + m)
                    e.tensor_scalar(t0, slot[:, 0:TV], cvw[:, wof:wof + 1], cvb[:, bof:bof + 1], ALU.mult, ALU.add)
                    e.scalar_tensor_tensor(t1, slot[:, 1:1 + TV], cvw[:, wof + 1:wof + 2], t0, ALU.mult, ALU.add)
                    e.scalar_tensor_tensor(t0, slot[:, 2:2 + TV], cvw[:, wof + 2:wof + 3], t1, ALU.mult, ALU.add)
                    c_cv.inc(e.scalar_tensor_tensor(t1, slot[:, 3:3 + TV], cvw[:, wof + 3:wof + 4], t0, ALU.mult, ALU.add))

            @b.scalar
            def _(e):
                for m in range(16):
                    if m >= 2:
                        e.wait_ge(SLOT[m % 2].sem, 16 * (m // 2))
                    e.wait_ge(c_cv.sem, cv0 + m + 1)
                    c_act.inc(e.activation(outb[:, (m % 2) * TV:(m % 2 + 1) * TV],
                                           ctmp[:, TV:2 * TV], AF.Silu))

            @b.gpsimd
            def _(e):
                for m in range(16):
                    isz = m >= 8
                    f = m - 8 if isz else m
                    dst = (zs_o if isz else xs_o)[128 * f:128 * (f + 1), :]
                    e.wait_ge(c_act.sem, act0 + m + 1)
                    SLOT[m % 2].inc(e.dma_start(out=dst, in_=outb[:, (m % 2) * TV:(m % 2 + 1) * TV]), 16)
                e.wait_ge(SLOT[0].sem, SLOT[0].n)
                e.wait_ge(SLOT[1].sem, SLOT[1].n)
    return nc


def host_preA_inputs(ins):
    x = ins["L0_attn_x"]
    onesm = np.ones((128, 128), BF)
    w_in = np.ascontiguousarray(ins["L0_attn_W_in"], np.float32)
    cw = np.ascontiguousarray(ins["L0_attn_conv_x_w"][:, 0, :].reshape(8, 128, 4).transpose(1, 0, 2).reshape(128, 32), np.float32)
    cb = np.ascontiguousarray(ins["L0_attn_conv_x_b"].reshape(8, 128).T, np.float32)
    zw = np.ascontiguousarray(ins["L0_attn_conv_z_w"][:, 0, :].reshape(8, 128, 4).transpose(1, 0, 2).reshape(128, 32), np.float32)
    zb = np.ascontiguousarray(ins["L0_attn_conv_z_b"].reshape(8, 128).T, np.float32)
    maps = []
    for c in range(NCORE):
        b, h = c // 2, c % 2
        xT = np.ascontiguousarray(x[b].T)
        xh = np.zeros((D, TL), np.float32)
        g0 = h * TV - 1
        lo = max(g0, 0)
        hi = min(g0 + TL, 4096)
        xh[:, lo - g0:hi - g0] = xT[:, lo:hi]
        maps.append(dict(xh=xh, w_in=w_in, cw=cw, cb=cb, zw=zw, zb=zb, onesm=onesm))
    return maps


PSP = [(0, 512), (512, 1024), (1024, 1536), (1536, 2048)]


def build_preB(stage="full"):
    nc = new_nc()
    xs_i = nc.dram_tensor("xs_i", [D, TV], BF16, kind="ExternalInput").ap()
    wq = nc.dram_tensor("wq", [D, 768], F32, kind="ExternalInput").ap()
    wk = nc.dram_tensor("wk", [D, 768], F32, kind="ExternalInput").ap()
    wv = nc.dram_tensor("wv", [D, 1536], F32, kind="ExternalInput").ap()
    wg = nc.dram_tensor("wg", [D, 1536], F32, kind="ExternalInput").ap()
    wgkb = nc.dram_tensor("wgkb", [128, 64], F32, kind="ExternalInput").ap()
    smalls = nc.dram_tensor("smalls", [4, 4], F32, kind="ExternalInput").ap()
    onesm = nc.dram_tensor("onesm", [128, 128], BF16, kind="ExternalInput").ap()
    oneslo = nc.dram_tensor("oneslo", [128, 128], BF16, kind="ExternalInput").ap()
    oneshi = nc.dram_tensor("oneshi", [128, 128], BF16, kind="ExternalInput").ap()
    ident = nc.dram_tensor("ident", [128, 128], BF16, kind="ExternalInput").ap()

    qn_o = nc.dram_tensor("qn_o", [768, TV], BF16, kind="ExternalOutput").ap()
    kn_o = nc.dram_tensor("kn_o", [768, TV], BF16, kind="ExternalOutput").ap()
    knt_o = nc.dram_tensor("knt_o", [6 * TV, 128], BF16, kind="ExternalOutput").ap()
    vt_o = nc.dram_tensor("vt_o", [TV, 1536], BF16, kind="ExternalOutput").ap()
    g_o = nc.dram_tensor("g_o", [1536, TV], BF16, kind="ExternalOutput").ap()
    gk_o = nc.dram_tensor("gk_o", [4, TV], F32, kind="ExternalOutput").ap()
    beta_o = nc.dram_tensor("beta_o", [4, TV], F32, kind="ExternalOutput").ap()
    dbg_o = nc.dram_tensor("dbg_o", [4, 2 * TV], F32, kind="ExternalOutput").ap()
    dbg2_o = nc.dram_tensor("dbg2_o", [128, 64], BF16, kind="ExternalOutput").ap()

    es = ExitStack()
    sb = lambda name, shape, dt: es.enter_context(nc.sbuf_tensor(name, shape, dt))
    ps = lambda name, shape, dt=F32: es.enter_context(nc.psum_tensor(name, shape, dt))

    Xs = sb("Xs", [128, NCH * TV], BF16)
    Wq = sb("Wq", [128, NCH * 768], BF16)
    Wk = sb("Wk", [128, NCH * 768], BF16)
    Wv = sb("Wv", [128, NCH * 1536], BF16)
    Wg = sb("Wg", [128, NCH * 1536], BF16)
    Wgkb = sb("Wgkb", [128, 64], F32)
    Wgkbb = sb("Wgkbb", [128, 64], BF16)
    wstg = sb("wstg", [128, 1536], F32)
    Qn = sb("Qn", [128, 6 * TV], BF16)
    rsqh = sb("rsqh", [128, 4 * 512], F32)
    som = sb("som", [128, 128], BF16)
    somlo = sb("somlo", [128, 128], BF16)
    somhi = sb("somhi", [128, 128], BF16)
    sid = sb("sid", [128, 128], BF16)
    sq6 = sb("sq6", [128, 6 * 512], BF16)
    eps6 = sb("eps6", [128, 1], F32)
    smal = sb("smal", [4, 4], F32)
    gkt = sb("gkt", [4, 2048], F32)
    gkt2 = sb("gkt2", [4, 2048], F32)
    gkt3 = sb("gkt3", [4, 2048], F32)
    ttmp = sb("ttmp", [128, 2 * 512], BF16)
    otmp = sb("otmp", [128, 2 * 512], BF16)
    negA = sb("negA", [4, 1], F32)

    PQ = [ps(f"PQ{i}", [128, 512]) for i in range(4)]
    PTb0 = ps("PTb0", [128, 128], BF16)
    PTb1 = ps("PTb1", [128, 128], BF16)
    PSm = ps("PSm", [4, 512])
    PSb = ps("PSb", [4, 512])

    c_ld = Ctr(es.enter_context(nc.semaphore("s_ld")))
    c_ldc = Ctr(es.enter_context(nc.semaphore("s_ldc")))
    c_cast = Ctr(es.enter_context(nc.semaphore("s_cast")))
    c_pe = Ctr(es.enter_context(nc.semaphore("s_pe")))
    c_dve = Ctr(es.enter_context(nc.semaphore("s_dve")))
    c_act = Ctr(es.enter_context(nc.semaphore("s_act")))
    c_st = Ctr(es.enter_context(nc.semaphore("s_st")))
    c_st0 = Ctr(es.enter_context(nc.semaphore("s_st0")))
    c_st1 = Ctr(es.enter_context(nc.semaphore("s_st1")))
    SLOT = [c_st0, c_st1]

    wlist = [("q", wq, Wq, 768), ("k", wk, Wk, 768), ("v", wv, Wv, 1536), ("g", wg, Wg, 1536)]

    with es:
        with nc.Block() as b:
            nfix = NCH + 6

            @b.sync
            def _(e):
                for f in range(NCH):
                    c_ldc.inc(e.dma_start(out=Xs[:, f * TV:(f + 1) * TV],
                                          in_=xs_i[128 * f:128 * (f + 1), :]), 16)
                c_ldc.inc(e.dma_start(out=Wgkb[:], in_=wgkb), 16)
                c_ldc.inc(e.dma_start(out=smal[:], in_=smalls), 16)
                c_ldc.inc(e.dma_start(out=som[:], in_=onesm), 16)
                c_ldc.inc(e.dma_start(out=sid[:], in_=ident), 16)
                c_ldc.inc(e.dma_start(out=somlo[:], in_=oneslo), 16)
                c_ldc.inc(e.dma_start(out=somhi[:], in_=oneshi), 16)
                if stage not in ("gkonly2",):
                    k = 0
                    for nm, src, dst, wd in wlist:
                        for f in range(NCH):
                            k += 1
                            if k >= 2 and stage not in ("gkonly3", "gkonly4"):
                                e.wait_ge(c_cast.sem, k - 1)
                            c_ld.inc(e.dma_start(out=wstg[:, 0:wd], in_=src[128 * f:128 * (f + 1), :]), 16)

            @b.gpsimd
            def _(e):
                e.memset(eps6[:], 1e-6)

            @b.vector
            def _(e):
                e.wait_ge(c_ldc.sem, 16 * (NCH + 6))
                e.tensor_copy(Wgkbb[:], Wgkb[:])
                if stage not in ("gkonly2", "gkonly3"):
                    k = 0
                    for nm, src, dst, wd in wlist:
                        for f in range(NCH):
                            k += 1
                            e.wait_ge(c_ld.sem, 16 * k)
                            c_cast.inc(e.tensor_copy(dst[:, f * wd:(f + 1) * wd], wstg[:, 0:wd]))

        if stage.startswith("gkonly"):
            passes = ()
        HP = [
            [(0, 0, 128, "all"), (1, 0, 64, "lo")],
            [(1, 64, 128, "hi"), (2, 0, 128, "all")],
            [(3, 0, 128, "all"), (4, 0, 64, "lo")],
            [(4, 64, 128, "hi"), (5, 0, 128, "all")],
        ]
        passes = ((Wq, qn_o, False),) if stage == 'q' else ((Wq, qn_o, False), (Wk, kn_o, True))
        for W, out_fm, do_tm in passes:
            with nc.Block() as b:
                pe0, dve0 = c_pe.n, c_dve.n

                @b.tensor
                def _(e, W=W):
                    e.wait_ge(c_ldc.sem, 16 * (NCH + 6))
                    k = 0
                    for m in range(6):
                        for si, (lo, hi) in enumerate(PSP):
                            if k >= 4:
                                e.wait_ge(c_dve.sem, dve0 + k - 3)
                            for f in range(NCH):
                                ins = e.matmul(PQ[k % 4][:, 0:512],
                                               W[:, f * 768 + m * 128:f * 768 + m * 128 + 128],
                                               Xs[:, f * TV + lo:f * TV + hi],
                                               start=(f == 0), stop=(f == NCH - 1))
                            c_pe.inc(ins)
                            k += 1

                @b.vector
                def _(e):
                    k = 0
                    for m in range(6):
                        for si, (lo, hi) in enumerate(PSP):
                            e.wait_ge(c_pe.sem, pe0 + k + 1)
                            c_dve.inc(e.tensor_copy(Qn[:, m * TV + lo:m * TV + hi], PQ[k % 4][:, 0:512]))
                            k += 1

            with nc.Block() as b:
                pe0, dve0, act0 = c_pe.n, c_dve.n, c_act.n
                OMK = {"all": som, "lo": somlo, "hi": somhi}

                @b.scalar
                def _(e):
                    for si, (lo, hi) in enumerate(PSP):
                        if si >= 1:
                            e.wait_ge(c_dve.sem, dve0 + si * 8)
                        for m in range(6):
                            c_act.inc(e.activation(sq6[:, m * 512:(m + 1) * 512],
                                                   Qn[:, m * TV + lo:m * TV + hi], AF.Square))
                        for h in range(4):
                            e.wait_ge(c_pe.sem, pe0 + si * 4 + h + 1)
                            e.activation(rsqh[:, h * 512:(h + 1) * 512], PQ[h][:, 0:512],
                                         AF.Ln, bias=eps6[:, 0:1], scale=1.0)
                            c_act.inc(e.activation(rsqh[:, h * 512:(h + 1) * 512],
                                                   rsqh[:, h * 512:(h + 1) * 512], AF.Exp, scale=-0.5))

                @b.tensor
                def _(e):
                    for si in range(4):
                        e.wait_ge(c_act.sem, act0 + si * 10 + 6)
                        for h in range(4):
                            if si >= 1:
                                e.wait_ge(c_act.sem, act0 + (si - 1) * 10 + 6 + h + 1)
                            for pi, (m, rl, rh, kind) in enumerate(HP[h]):
                                ins = e.matmul(PQ[h][:, 0:512], OMK[kind][:],
                                               sq6[:, m * 512:(m + 1) * 512],
                                               start=(pi == 0), stop=(pi == len(HP[h]) - 1))
                            c_pe.inc(ins)

                @b.vector
                def _(e):
                    for si, (lo, hi) in enumerate(PSP):
                        for h in range(4):
                            e.wait_ge(c_act.sem, act0 + si * 10 + 6 + h + 1)
                            for (m, rl, rh, kind) in HP[h]:
                                c_dve.inc(e.tensor_tensor(
                                    Qn[rl:rh, m * TV + lo:m * TV + hi],
                                    Qn[rl:rh, m * TV + lo:m * TV + hi],
                                    rsqh[rl:rh, h * 512:(h + 1) * 512], ALU.mult))
                        c_dve.n -= 8
                        c_dve.n += 8

            with nc.Block() as b:
                st0 = c_st.n

                @b.gpsimd
                def _(e, out_fm=out_fm):
                    for m in range(6):
                        c_st.inc(e.dma_start(out=out_fm[m * 128:(m + 1) * 128, :],
                                             in_=Qn[:, m * TV:(m + 1) * TV]), 16)
                    e.wait_ge(c_st.sem, c_st.n)

            if do_tm and stage not in ("q", "qk_notm"):
                with nc.Block() as b:
                    pe0, dve0, st0 = c_pe.n, c_dve.n, c_st.n

                    @b.tensor
                    def _(e):
                        t = 0
                        for m in range(6):
                            for tt in range(16):
                                if t >= 2:
                                    e.wait_ge(c_dve.sem, dve0 + t - 1)
                                c_pe.inc(e.transpose((PTb0 if t % 2 == 0 else PTb1)[:],
                                                     Qn[:, m * TV + tt * 128:m * TV + tt * 128 + 128],
                                                     sid[:]))
                                t += 1

                    sslot0 = [SLOT[0].n, SLOT[1].n]

                    @b.vector
                    def _(e):
                        t = 0
                        for m in range(6):
                            for tt in range(16):
                                e.wait_ge(c_pe.sem, pe0 + t + 1)
                                if t >= 2:
                                    e.wait_ge(SLOT[t % 2].sem, sslot0[t % 2] + 16 * (t // 2))
                                c_dve.inc(e.tensor_copy(ttmp[:, (t % 2) * 512:(t % 2) * 512 + 128],
                                                        (PTb0 if t % 2 == 0 else PTb1)[:]))
                                t += 1

                    @b.gpsimd
                    def _(e):
                        t = 0
                        for m in range(6):
                            for tt in range(16):
                                e.wait_ge(c_dve.sem, dve0 + t + 1)
                                SLOT[t % 2].inc(e.dma_start(out=knt_o[t * 128:(t + 1) * 128, :],
                                                            in_=ttmp[:, (t % 2) * 512:(t % 2) * 512 + 128]), 16)
                                t += 1
                        e.wait_ge(SLOT[0].sem, SLOT[0].n)
                        e.wait_ge(SLOT[1].sem, SLOT[1].n)

        if stage in ("qk", "qk_notm"):
            es.pop_all()
            return nc
        if not stage.startswith("gkonly"):
            with nc.Block() as b:
                pe0, dve0, st0 = c_pe.n, c_dve.n, c_st.n

                @b.tensor
                def _(e):
                    k = 0
                    for tt in range(16):
                        for sp in range(3):
                            if k >= 4:
                                e.wait_ge(c_dve.sem, dve0 + k - 3)
                            for f in range(NCH):
                                ins = e.matmul(PQ[k % 4][:, 0:512],
                                               Xs[:, f * TV + tt * 128:f * TV + tt * 128 + 128],
                                               Wv[:, f * 1536 + sp * 512:f * 1536 + sp * 512 + 512],
                                               start=(f == 0), stop=(f == NCH - 1))
                            c_pe.inc(ins)
                            k += 1
                    for m in range(12):
                        for si, (lo, hi) in enumerate(PSP):
                            if k >= 4:
                                e.wait_ge(c_dve.sem, dve0 + k - 3)
                            for f in range(NCH):
                                ins = e.matmul(PQ[k % 4][:, 0:512],
                                               Wg[:, f * 1536 + m * 128:f * 1536 + m * 128 + 128],
                                               Xs[:, f * TV + lo:f * TV + hi],
                                               start=(f == 0), stop=(f == NCH - 1))
                            c_pe.inc(ins)
                            k += 1

                @b.vector
                def _(e):
                    k = 0
                    for tt in range(16):
                        for sp in range(3):
                            e.wait_ge(c_pe.sem, pe0 + k + 1)
                            if k >= 2:
                                e.wait_ge(c_st.sem, st0 + 16 * (k - 1))
                            c_dve.inc(e.tensor_copy(otmp[:, (k % 2) * 512:(k % 2 + 1) * 512], PQ[k % 4][:, 0:512]))
                            k += 1
                    for m in range(12):
                        for si, (lo, hi) in enumerate(PSP):
                            e.wait_ge(c_pe.sem, pe0 + k + 1)
                            if k >= 2:
                                e.wait_ge(c_st.sem, st0 + 16 * (k - 1))
                            c_dve.inc(e.tensor_copy(otmp[:, (k % 2) * 512:(k % 2 + 1) * 512], PQ[k % 4][:, 0:512]))
                            k += 1

                @b.gpsimd
                def _(e):
                    k = 0
                    for tt in range(16):
                        for sp in range(3):
                            e.wait_ge(c_dve.sem, dve0 + k + 1)
                            c_st.inc(e.dma_start(out=vt_o[tt * 128:(tt + 1) * 128, sp * 512:(sp + 1) * 512],
                                                 in_=otmp[:, (k % 2) * 512:(k % 2 + 1) * 512]), 16)
                            k += 1
                    for m in range(12):
                        for si, (lo, hi) in enumerate(PSP):
                            e.wait_ge(c_dve.sem, dve0 + k + 1)
                            c_st.inc(e.dma_start(out=g_o[m * 128:(m + 1) * 128, lo:hi],
                                                 in_=otmp[:, (k % 2) * 512:(k % 2 + 1) * 512]), 16)
                            k += 1


        if stage == "vg":
            es.pop_all()
            return nc
        with nc.Block() as b:
            pe0, dve0, act0 = c_pe.n, c_dve.n, c_act.n
            st0_b5 = c_st.n

            @b.tensor
            def _(e):
                e.wait_ge(c_ldc.sem, 16 * (NCH + 6))
                for si, (lo, hi) in enumerate(PSP):
                    if si >= 1:
                        e.wait_ge(c_dve.sem, dve0 + 2 * si)
                    for f in range(NCH):
                        e.matmul(PSm[:, 0:512], Wgkbb[:, f * 8:f * 8 + 4],
                                 Xs[:, f * TV + lo:f * TV + hi],
                                 start=(f == 0), stop=(f == NCH - 1))
                    for f in range(NCH):
                        ins = e.matmul(PSb[:, 0:512], Wgkbb[:, f * 8 + 4:f * 8 + 8],
                                       Xs[:, f * TV + lo:f * TV + hi],
                                       start=(f == 0), stop=(f == NCH - 1))
                    c_pe.inc(ins)

            @b.vector
            def _(e):
                for si, (lo, hi) in enumerate(PSP):
                    e.wait_ge(c_pe.sem, pe0 + si + 1)
                    c_dve.inc(e.tensor_scalar(gkt[:, lo:hi], PSm[:, 0:512], smal[:, 2:3], None, ALU.add))
                    c_dve.inc(e.tensor_copy(gkt3[:, lo:hi], PSb[:, 0:512]))
                e.wait_ge(c_act.sem, act0 + 2)
                c_dve.inc(e.tensor_tensor(gkt2[:], gkt2[:], gkt[:], ALU.add))
                e.wait_ge(c_act.sem, act0 + 3)
                c_dve.inc(e.tensor_scalar(gkt2[:], gkt2[:], negA[:, 0:1], -1.0, ALU.mult, ALU.mult))

            @b.scalar
            def _(e):
                e.wait_ge(c_dve.sem, dve0 + 8)
                e.wait_ge(c_st.sem, st0_b5 + 48)
                e.activation(negA[:], smal[:, 1:2], AF.Exp, scale=1.0)
                e.activation(gkt2[:], gkt[:], AF.Abs)
                e.activation(gkt2[:], gkt2[:], AF.Exp, scale=-1.0)
                c_act.inc(e.activation(gkt2[:], gkt2[:], AF.Ln, bias=smal[:, 3:4], scale=1.0))
                c_act.inc(e.activation(gkt[:], gkt[:], AF.Relu))
                c_act.inc(e.activation(gkt3[:], gkt3[:], AF.Sigmoid, bias=smal[:, 0:1], scale=1.0))

            @b.gpsimd
            def _(e):
                e.wait_ge(c_dve.sem, dve0 + 8)
                c_st.inc(e.dma_start(out=dbg2_o, in_=Wgkbb[:]), 16)
                c_st.inc(e.dma_start(out=dbg_o[:, 0:TV], in_=gkt[0:4, :]), 16)
                c_st.inc(e.dma_start(out=dbg_o[:, TV:2 * TV], in_=gkt3[0:4, :]), 16)
                e.wait_ge(c_st.sem, c_st.n)
                e.then_hold if False else None
                e.wait_ge(c_dve.sem, c_dve.n)
                c_st.inc(e.dma_start(out=gk_o, in_=gkt2[0:4, :]), 16)
                e.wait_ge(c_act.sem, c_act.n)
                c_st.inc(e.dma_start(out=beta_o, in_=gkt3[0:4, :]), 16)
                e.wait_ge(c_st.sem, c_st.n)
    return nc


def host_preB_inputs(ins, xs_list):
    wgkb = np.zeros((128, 64), np.float32)
    for f in range(8):
        wgkb[:, f * 8:f * 8 + 4] = ins["L0_attn_W_gk"][f * 128:(f + 1) * 128, :]
        wgkb[:, f * 8 + 4:f * 8 + 8] = ins["L0_attn_W_b"][f * 128:(f + 1) * 128, :]
    smalls = np.zeros((4, 4), np.float32)
    smalls[:, 0] = ins["L0_attn_b_b"]
    smalls[:, 1] = ins["L0_attn_A_log"]
    smalls[:, 2] = ins["L0_attn_dt_bias"]
    smalls[:, 3] = 1.0
    onesm = np.ones((128, 128), BF)
    oneslo = np.zeros((128, 128), BF); oneslo[0:64, :] = 1
    oneshi = np.zeros((128, 128), BF); oneshi[64:128, :] = 1
    ident = np.eye(128, dtype=BF)
    com = dict(wq=np.ascontiguousarray(ins["L0_attn_W_q"], np.float32),
               wk=np.ascontiguousarray(ins["L0_attn_W_k"], np.float32),
               wv=np.ascontiguousarray(ins["L0_attn_W_v"], np.float32),
               wg=np.ascontiguousarray(ins["L0_attn_W_g"], np.float32),
               wgkb=wgkb, smalls=smalls, onesm=onesm, oneslo=oneslo, oneshi=oneshi, ident=ident)
    return [dict(xs_i=np.ascontiguousarray(xs_list[c]), **com) for c in range(NCORE)]


NCHK = 32
PE_INC = 22
DVE_INC = 25
ACT_INC = 4


def build_delta(nchk=NCHK, nstr=2, trunc=99):
    nc = new_nc()
    qf = nc.dram_tensor("qf", [2, 192, 4096], BF16, kind="ExternalInput").ap()
    kf = nc.dram_tensor("kf", [2, 192, 4096], BF16, kind="ExternalInput").ap()
    kt = nc.dram_tensor("kt", [2, 4096, 192], BF16, kind="ExternalInput").ap()
    vt = nc.dram_tensor("vt", [2, 4096, 384], BF16, kind="ExternalInput").ap()
    gkr = nc.dram_tensor("gkr", [2, 32, 128], F32, kind="ExternalInput").ap()
    betar = nc.dram_tensor("betar", [2, 32, 128], F32, kind="ExternalInput").ap()
    maskui = nc.dram_tensor("maskui", [128, 128], F32, kind="ExternalInput").ap()
    maskls = nc.dram_tensor("maskls", [128, 128], F32, kind="ExternalInput").ap()
    identb = nc.dram_tensor("identb", [128, 128], BF16, kind="ExternalInput").ap()
    identf = nc.dram_tensor("identf", [128, 128], F32, kind="ExternalInput").ap()
    onesrow = nc.dram_tensor("onesrow", [1, 128], F32, kind="ExternalInput").ap()
    onesmb = nc.dram_tensor("onesmb", [128, 128], BF16, kind="ExternalInput").ap()
    onescol = nc.dram_tensor("onescol", [128, 1], F32, kind="ExternalInput").ap()
    on_o = nc.dram_tensor("on_o", [2, 384, 4096], BF16, kind="ExternalOutput").ap()

    es = ExitStack()
    sb = lambda name, shape, dt: es.enter_context(nc.sbuf_tensor(name, shape, dt))
    ps = lambda name, shape, dt=F32: es.enter_context(nc.psum_tensor(name, shape, dt))

    MUI = sb("MUI", [128, 128], F32)
    MLS = sb("MLS", [128, 128], F32)
    Ib = sb("Ib", [128, 128], BF16)
    If = sb("If", [128, 128], F32)
    OR1 = sb("OR1", [1, 128], F32)
    OMB = sb("OMB", [128, 128], BF16)
    OC1 = sb("OC1", [128, 1], F32)
    eps5 = sb("eps5", [128, 1], F32)
    Gk = sb("Gk", [32, 2 * 128], F32)
    Bt = sb("Bt", [32, 2 * 128], F32)
    Gcum = sb("Gcum", [32, 2 * 128], F32)
    ccols = sb("ccols", [128, 2 * 32], F32)
    bcols = sb("bcols", [128, 2 * 32], F32)
    nbcols = sb("nbcols", [128, 2 * 32], F32)
    btok = sb("btok", [128, 2 * 32], F32)
    nbtok = sb("nbtok", [128, 2 * 32], F32)
    explast = sb("explast", [128, 2 * 32], F32)
    bccols = sb("bccols", [128, 2 * 32], F32)
    clrow = sb("clrow", [1, 2 * 32], F32)
    qA = sb("qA", [128, 2 * 128], BF16)
    qB = sb("qB", [64, 2 * 128], BF16)
    kA = sb("kA", [128, 2 * 128], BF16)
    kB = sb("kB", [64, 2 * 128], BF16)
    ktj = sb("ktj", [128, 2 * 192], BF16)
    vtj = sb("vtj", [128, 2 * 384], BF16)
    crow = sb("crow", [1, 128], F32)
    brow = sb("brow", [1, 128], F32)
    M1 = sb("M1", [128, 128], F32)
    M2 = sb("M2", [128, 128], F32)
    DupF = sb("DupF", [128, 128], F32)
    DloF = sb("DloF", [128, 128], F32)
    DupI = sb("DupI", [128, 128], F32)
    DloS = sb("DloS", [128, 128], F32)
    qtA = sb("qtA", [128, 128], BF16)
    qtB = sb("qtB", [64, 128], BF16)
    KKs = sb("KKs", [128, 128], BF16)
    Aqk = sb("Aqk", [128, 128], BF16)
    Nsb = sb("Nsb", [128, 128], BF16)
    NTs = sb("NTs", [128, 128], BF16)
    Abuf = sb("Abuf", [64, 2 * 64], BF16)
    ATbuf = sb("ATbuf", [64, 2 * 64], BF16)
    PTbuf = sb("PTbuf", [64, 2 * 2 * 64], BF16)
    N2sb = sb("N2sb", [64, 64], BF16)
    NT2sb = sb("NT2sb", [64, 64], BF16)
    Wsb = sb("Wsb", [64, 384], BF16)
    Rt = sb("Rt", [128, 384], F32)
    Rbf = sb("Rbf", [128, 384], BF16)
    Usb = sb("Usb", [128, 384], BF16)
    Khat = sb("Khat", [128, 192], BF16)
    osq = sb("osq", [128, 384], BF16)
    osb = sb("osb", [128, 384], F32)
    rstd = sb("rstd", [128, 128], F32)
    SfA = sb("SfA", [128, 2 * 2 * 384], F32)
    SfB = sb("SfB", [64, 2 * 2 * 384], F32)
    SbA = sb("SbA", [128, 2 * 384], BF16)
    SbB = sb("SbB", [64, 2 * 384], BF16)
    Ob = sb("Ob", [128, 2 * 3 * 4096], BF16)

    PMISC = ps("PMISC", [128, 512])
    PKQ = ps("PKQ", [128, 256])
    PIA = ps("PIA", [64, 128])
    PIP = ps("PIP", [64, 64])
    PNT = ps("PNT", [128, 128], BF16)
    PBIG = ps("PBIG", [128, 384])
    PR2 = ps("PR2", [128, 128])
    PSB = ps("PSB", [64, 384])

    c_ldc = Ctr(es.enter_context(nc.semaphore("s_ldc")))
    c_lq = [Ctr(es.enter_context(nc.semaphore("s_lq0"))),
            Ctr(es.enter_context(nc.semaphore("s_lq1")))]
    c_pe = Ctr(es.enter_context(nc.semaphore("s_pe")))
    c_dve = Ctr(es.enter_context(nc.semaphore("s_dve")))
    c_act = Ctr(es.enter_context(nc.semaphore("s_act")))
    c_st = Ctr(es.enter_context(nc.semaphore("s_st")))

    with es:
        with nc.Block() as b:
            @b.sync
            def _(e):
                for dst, src in ((MUI, maskui), (MLS, maskls), (Ib, identb), (If, identf),
                                 (OR1, onesrow), (OMB, onesmb), (OC1, onescol)):
                    c_ldc.inc(e.dma_start(out=dst[:], in_=src), 16)
                for s in range(2):
                    c_ldc.inc(e.dma_start(out=Gk[:, s * 128:(s + 1) * 128], in_=gkr[s]), 16)
                    c_ldc.inc(e.dma_start(out=Bt[:, s * 128:(s + 1) * 128], in_=betar[s]), 16)
            NLDC = c_ldc.n

            @b.gpsimd
            def _(e):
                e.memset(eps5[:], 1e-5)
                e.memset(SfA[:], 0.0)
                e.memset(SfB[:], 0.0)
                e.memset(SbA[:], 0.0)
                e.memset(SbB[:], 0.0)
                e.memset(Ob[:], 0.0)

            @b.vector
            def _(e):
                e.wait_ge(c_ldc.sem, NLDC)
                for s in range(2):
                    sl = slice(s * 128, (s + 1) * 128)
                    e.tensor_tensor_scan(Gcum[:, sl], OC1[0:32, 0:1].broadcast_to([32, 128]),
                                         Gk[:, sl], 0.0, ALU.mult, ALU.add)
                c_dve.inc(e.engine_nop() if False else e.tensor_scalar(nbcols[:, 0:1], OC1[:, 0:1], 1.0, None, ALU.mult))

            @b.tensor
            def _(e):
                e.wait_ge(c_dve.sem, 1)
                e.wait_ge(c_ldc.sem, NLDC)
                for s in range(2):
                    sl = slice(s * 128, (s + 1) * 128)
                    sc = slice(s * 32, (s + 1) * 32)
                    e.transpose(PMISC[0:128, 0:32], Gcum[:, sl], If[0:32, 0:32])
                    e.transpose(PMISC[0:128, 32:64], Bt[:, sl], If[0:32, 0:32])
                    c_pe.inc(e.transpose(PMISC[0:1, 64:96], Gcum[:, s * 128 + 127:s * 128 + 128], If[0:32, 0:32]))

                    e.wait_ge(c_dve.sem, 2 + s)

            @b.vector
            def _(e):
                for s in range(2):
                    sc = slice(s * 32, (s + 1) * 32)
                    e.wait_ge(c_pe.sem, s + 1)
                    e.tensor_copy(ccols[:, sc], PMISC[0:128, 0:32])
                    e.tensor_copy(bcols[:, sc], PMISC[0:128, 32:64])
                    e.tensor_copy(clrow[0:1, sc], PMISC[0:1, 64:96])
                    c_dve.inc(e.tensor_scalar(nbcols[:, sc], bcols[:, sc], -1.0, None, ALU.mult))

        with nc.Block() as b:
            @b.tensor
            def _(e):
                c_pe.inc(e.matmul(PMISC[:, 0:64], OR1[:], clrow[:], start=True, stop=True))

            @b.vector
            def _(e):
                e.wait_ge(c_pe.sem, c_pe.n)
                e.tensor_copy(bccols[:, 0:64], PMISC[:, 0:64])
                c_dve.inc(e.tensor_sub(explast[:, 0:64], bccols[:, 0:64], ccols[:, 0:64]))

            @b.scalar
            def _(e):
                e.wait_ge(c_dve.sem, c_dve.n)
                e.activation(btok[:, 0:64], ccols[:, 0:64], AF.Exp)
                e.activation(explast[:, 0:64], explast[:, 0:64], AF.Exp)
                c_act.inc(e.activation(bccols[:, 0:64], bccols[:, 0:64], AF.Exp))

            @b.vector
            def _(e):
                e.wait_ge(c_act.sem, c_act.n)
                c_dve.inc(e.tensor_scalar(nbtok[:, 0:64], btok[:, 0:64], -1.0, None, ALU.mult))

        PE0, DVE0, ACT0 = c_pe.n, c_dve.n, c_act.n
        pe_inc = 32
        dve_inc = 35
        PE_KS, PE_UT, PE_WB, PE_UB, PE_O, PE_R2, PE_S = 6, 17, 28, 29, 30, 31, 32
        D_R, D_UT, D_W, D_UB, D_KH, D_O, D_ON, D_S = 8, 19, 30, 31, 32, 33, 34, 35
        ACT_I = 4

        def pem(u, k):
            return PE0 + u * pe_inc + k

        def dvem(u, k):
            return DVE0 + u * dve_inc + k

        def actm(u, k):
            return ACT0 + u * ACT_I + k

        with nc.Block() as b:
            @b.sync
            def _(e):
                for u in range(nstr * nchk):
                    s, j = divmod(u, nchk)
                    sl = u % 2
                    if u >= 2 and trunc >= 7:
                        e.wait_ge(c_dve.sem, dvem(u - 2, D_KH))
                    c_lq[sl].inc(e.dma_start(out=qA[:, sl * 128:(sl + 1) * 128], in_=qf[s, 0:128, j * 128:(j + 1) * 128]), 16)
                    c_lq[sl].inc(e.dma_start(out=qB[:, sl * 128:(sl + 1) * 128], in_=qf[s, 128:192, j * 128:(j + 1) * 128]), 16)
                    c_lq[sl].inc(e.dma_start(out=kA[:, sl * 128:(sl + 1) * 128], in_=kf[s, 0:128, j * 128:(j + 1) * 128]), 16)
                    c_lq[sl].inc(e.dma_start(out=kB[:, sl * 128:(sl + 1) * 128], in_=kf[s, 128:192, j * 128:(j + 1) * 128]), 16)
                    c_lq[sl].inc(e.dma_start(out=ktj[:, sl * 192:(sl + 1) * 192], in_=kt[s, j * 128:(j + 1) * 128, :]), 16)
                    c_lq[sl].inc(e.dma_start(out=vtj[:, sl * 384:(sl + 1) * 384], in_=vt[s, j * 128:(j + 1) * 128, :]), 16)

            @b.tensor
            def _(e):
                for u in range(nstr * nchk):
                    s, j = divmod(u, nchk)
                    sl = u % 2
                    jc = s * 32 + j
                    if u >= 1 and trunc >= 7:
                        e.wait_ge(c_dve.sem, dvem(u - 1, dve_inc))
                        e.wait_ge(c_act.sem, actm(u - 1, ACT_I))
                    e.transpose(PMISC[0:1, 256:384], ccols[:, jc:jc + 1], If[:])
                    c_pe.inc(e.transpose(PMISC[0:1, 384:512], btok[:, jc:jc + 1], If[:]))
                    e.wait_ge(c_dve.sem, dvem(u, 1))
                    e.matmul(PMISC[:, 0:128], OR1[:], crow[:], start=True, stop=True)
                    c_pe.inc(e.matmul(PMISC[:, 128:256], OR1[:], brow[:], start=True, stop=True))
                    if trunc < 2:
                        continue
                    e.wait_ge(c_lq[sl].sem, 96 * (u // 2 + 1))
                    e.matmul(PKQ[:, 0:128], kA[:, sl * 128:(sl + 1) * 128], kA[:, sl * 128:(sl + 1) * 128], start=True, stop=False)
                    c_pe.inc(e.matmul(PKQ[:, 0:128], kB[:, sl * 128:(sl + 1) * 128], kB[:, sl * 128:(sl + 1) * 128], start=False, stop=True))
                    e.matmul(PKQ[:, 128:256], kA[:, sl * 128:(sl + 1) * 128], qA[:, sl * 128:(sl + 1) * 128], start=True, stop=False)
                    c_pe.inc(e.matmul(PKQ[:, 128:256], kB[:, sl * 128:(sl + 1) * 128], qB[:, sl * 128:(sl + 1) * 128], start=False, stop=True))
                    e.wait_ge(c_dve.sem, dvem(u, 6))
                    c_pe.inc(e.transpose(PNT[:], Nsb[:], Ib[:]))
                    e.matmul(PBIG[:], kA[:, sl * 128:(sl + 1) * 128], SbA[:, s * 384:(s + 1) * 384], start=True, stop=False)
                    c_pe.inc(e.matmul(PBIG[:], kB[:, sl * 128:(sl + 1) * 128], SbB[:, s * 384:(s + 1) * 384], start=False, stop=True))
                    if trunc < 3:
                        continue
                    for bi in range(2):
                        if bi == 0:
                            Nd, NTd = Nsb[0:64, 0:64], NTs[0:64, 0:64]
                        else:
                            Nd, NTd = N2sb[:], NT2sb[:]
                        for lv in range(5):
                            pa = (lv + 1) % 2
                            if lv == 0:
                                e.wait_ge(c_dve.sem, dvem(u, 7 if bi == 0 else 17))
                                Acur, ATcur = Nd, NTd
                            else:
                                e.wait_ge(c_dve.sem, dvem(u, (7 if bi == 0 else 18) + 2 * lv))
                                Acur = Abuf[:, (lv % 2) * 64:(lv % 2) * 64 + 64]
                                ATcur = ATbuf[:, (lv % 2) * 64:(lv % 2) * 64 + 64]
                            e.matmul(PIA[:, 0:64], ATcur, Acur, start=True, stop=True)
                            c_pe.inc(e.matmul(PIA[:, 64:128], Acur, ATcur, start=True, stop=True))
                            e.wait_ge(c_dve.sem, dvem(u, (9 if bi == 0 else 20) + 2 * lv))
                            Anew = Abuf[:, pa * 64:pa * 64 + 64]
                            ATnew = ATbuf[:, pa * 64:pa * 64 + 64]
                            PTcur = NTd if lv == 0 else PTbuf[:, (bi * 2 + lv % 2) * 64:(bi * 2 + lv % 2) * 64 + 64]
                            e.matmul(PIP[:], Anew, PTcur, start=True, stop=False)
                            e.matmul(PIP[:], Ib[0:64, 0:64], ATnew, start=False, stop=False)
                            c_pe.inc(e.matmul(PIP[:], Ib[0:64, 0:64], PTcur, start=False, stop=True))
                        if bi == 0:
                            e.wait_ge(c_dve.sem, dvem(u, 18))
                            PT1 = PTbuf[:, 1 * 64:2 * 64]
                            e.matmul(PBIG[0:64, :], PT1, Rbf[0:64, :], start=True, stop=False)
                            c_pe.inc(e.matmul(PBIG[0:64, :], Ib[0:64, 0:64], Rbf[0:64, :], start=False, stop=True))
                    if trunc < 4:
                        continue
                    e.wait_ge(c_dve.sem, dvem(u, D_UT))
                    e.matmul(PBIG[0:64, :], Ib[:, 64:128], Rbf[:], start=True, stop=False)
                    c_pe.inc(e.matmul(PBIG[0:64, :], NTs[0:64, 64:128], Usb[0:64, :], start=False, stop=True))
                    e.wait_ge(c_dve.sem, dvem(u, D_W))
                    PT2 = PTbuf[:, 3 * 64:4 * 64]
                    e.matmul(PBIG[0:64, :], PT2, Wsb[:], start=True, stop=False)
                    c_pe.inc(e.matmul(PBIG[0:64, :], Ib[0:64, 0:64], Wsb[:], start=False, stop=True))
                    if trunc < 5:
                        continue
                    e.wait_ge(c_dve.sem, dvem(u, D_UB))
                    for m in range(3):
                        e.matmul(PBIG[:, m * 128:(m + 1) * 128], SbA[:, s * 384 + m * 128:s * 384 + m * 128 + 128],
                                 qtA[:], start=True, stop=False)
                        e.matmul(PBIG[:, m * 128:(m + 1) * 128], Usb[:, m * 128:(m + 1) * 128],
                                 Aqk[:], start=False, stop=False)
                        ins = e.matmul(PBIG[:, m * 128:(m + 1) * 128], SbB[:, s * 384 + m * 128:s * 384 + m * 128 + 128],
                                 qtB[:], start=False, stop=True)
                    c_pe.inc(ins)
                    if trunc < 7:
                        continue
                    e.wait_ge(c_act.sem, actm(u, 3))
                    for m in range(3):
                        ins = e.matmul(PR2[:], OMB[:], osq[:, m * 128:(m + 1) * 128],
                                       start=(m == 0), stop=(m == 2))
                    c_pe.inc(ins)
                    e.wait_ge(c_dve.sem, dvem(u, D_O))
                    e.matmul(PBIG[:], Khat[:, 0:128], Usb[:], start=True, stop=True)
                    c_pe.inc(e.matmul(PSB[:], Khat[:, 128:192], Usb[:], start=True, stop=True))

            @b.vector
            def _(e):
                for u in range(nstr * nchk):
                    s, j = divmod(u, nchk)
                    sl = u % 2
                    pp = j % 2
                    jc = s * 32 + j
                    ccol = ccols[:, jc:jc + 1]
                    e.wait_ge(c_pe.sem, pem(u, 1))
                    e.tensor_copy(crow[:], PMISC[0:1, 256:384])
                    c_dve.inc(e.tensor_copy(brow[:], PMISC[0:1, 384:512]))
                    e.wait_ge(c_pe.sem, pem(u, 2))
                    c_dve.inc(e.tensor_scalar(M1[:], PMISC[:, 0:128], ccol, 0.0, ALU.subtract, ALU.min))
                    e.tensor_scalar(M2[:], PMISC[:, 0:128], ccol, -1.0, ALU.subtract, ALU.mult)
                    c_dve.inc(e.tensor_scalar(M2[:], M2[:], 0.0, None, ALU.min))
                    if trunc < 2:
                        continue
                    e.wait_ge(c_act.sem, actm(u, 1))
                    e.tensor_mul(DupI[:], DupF[:], MUI[:])
                    e.wait_ge(c_act.sem, actm(u, 2))
                    e.tensor_mul(DloS[:], DloF[:], MLS[:])
                    e.tensor_mul(qtA[:], qA[:, sl * 128:(sl + 1) * 128], PMISC[:, 128:256])
                    c_dve.inc(e.tensor_mul(qtB[:], qB[:, sl * 128:(sl + 1) * 128], PMISC[0:64, 128:256]))
                    e.wait_ge(c_pe.sem, pem(u, 4))
                    e.tensor_copy(KKs[:], PKQ[:, 0:128])
                    c_dve.inc(e.tensor_mul(Aqk[:], DupI[:], PKQ[:, 128:256]))
                    c_dve.inc(e.scalar_tensor_tensor(Nsb[:], DloS[:], nbcols[:, jc:jc + 1], KKs[:], ALU.mult, ALU.mult))
                    e.wait_ge(c_pe.sem, pem(u, 5))
                    e.tensor_copy(NTs[:], PNT[:])
                    e.tensor_copy(N2sb[:], Nsb[64:128, 64:128])
                    c_dve.inc(e.tensor_copy(NT2sb[:], NTs[64:128, 64:128]))
                    e.wait_ge(c_pe.sem, pem(u, PE_KS))
                    e.scalar_tensor_tensor(Rt[:], PBIG[:], nbtok[:, jc:jc + 1], vtj[:, sl * 384:(sl + 1) * 384], ALU.mult, ALU.add)
                    c_dve.inc(e.tensor_scalar(Rbf[:], Rt[:], bcols[:, jc:jc + 1], None, ALU.mult))
                    if trunc < 3:
                        continue
                    for bi in range(2):
                        for lv in range(5):
                            pa = (lv + 1) % 2
                            e.wait_ge(c_pe.sem, pem(u, (7 if bi == 0 else 18) + 2 * lv))
                            e.tensor_copy(Abuf[:, pa * 64:pa * 64 + 64], PIA[:, 0:64])
                            c_dve.inc(e.tensor_copy(ATbuf[:, pa * 64:pa * 64 + 64], PIA[:, 64:128]))
                            e.wait_ge(c_pe.sem, pem(u, (8 if bi == 0 else 19) + 2 * lv))
                            c_dve.inc(e.tensor_copy(PTbuf[:, (bi * 2 + pa) * 64:(bi * 2 + pa) * 64 + 64], PIP[:]))
                        if bi == 0:
                            e.wait_ge(c_pe.sem, pem(u, PE_UT))
                            c_dve.inc(e.tensor_copy(Usb[0:64, :], PBIG[0:64, :]))
                    if trunc < 4:
                        continue
                    e.wait_ge(c_pe.sem, pem(u, PE_WB))
                    c_dve.inc(e.tensor_copy(Wsb[:], PBIG[0:64, :]))
                    e.wait_ge(c_pe.sem, pem(u, PE_UB))
                    c_dve.inc(e.tensor_copy(Usb[64:128, :], PBIG[0:64, :]))
                    c_dve.inc(e.tensor_scalar(Khat[:], ktj[:, sl * 192:(sl + 1) * 192], explast[:, jc:jc + 1], None, ALU.mult))
                    if trunc < 6:
                        continue
                    e.wait_ge(c_pe.sem, pem(u, PE_O))
                    c_dve.inc(e.tensor_copy(osb[:], PBIG[:]))
                    if trunc < 7:
                        continue
                    e.wait_ge(c_act.sem, actm(u, 4))
                    for m in range(3):
                        ins = e.tensor_mul(Ob[:, (s * 3 + m) * 4096 + j * 128:(s * 3 + m) * 4096 + j * 128 + 128],
                                           osb[:, m * 128:(m + 1) * 128], rstd[:])
                    c_dve.inc(ins)
                    e.wait_ge(c_pe.sem, pem(u, PE_S))
                    npp = (j + 1) % 2
                    e.scalar_tensor_tensor(SfA[:, (s * 2 + npp) * 384:(s * 2 + npp) * 384 + 384],
                                           SfA[:, (s * 2 + pp) * 384:(s * 2 + pp) * 384 + 384],
                                           bccols[:, jc:jc + 1], PBIG[:], ALU.mult, ALU.add)
                    e.scalar_tensor_tensor(SfB[:, (s * 2 + npp) * 384:(s * 2 + npp) * 384 + 384],
                                           SfB[:, (s * 2 + pp) * 384:(s * 2 + pp) * 384 + 384],
                                           bccols[0:64, jc:jc + 1], PSB[:], ALU.mult, ALU.add)
                    e.tensor_copy(SbA[:, s * 384:(s + 1) * 384], SfA[:, (s * 2 + npp) * 384:(s * 2 + npp) * 384 + 384])
                    c_dve.inc(e.tensor_copy(SbB[:, s * 384:(s + 1) * 384], SfB[:, (s * 2 + npp) * 384:(s * 2 + npp) * 384 + 384]))

            @b.scalar
            def _(e):
                for u in range(nstr * nchk):
                    e.wait_ge(c_dve.sem, dvem(u, 2))
                    c_act.inc(e.activation(DupF[:], M1[:], AF.Exp))
                    e.wait_ge(c_dve.sem, dvem(u, 3))
                    c_act.inc(e.activation(DloF[:], M2[:], AF.Exp))
                    if trunc < 6:
                        continue
                    e.wait_ge(c_dve.sem, dvem(u, D_O))
                    c_act.inc(e.activation(osq[:], osb[:], AF.Square))
                    if trunc < 7:
                        continue
                    e.wait_ge(c_pe.sem, pem(u, PE_R2))
                    e.activation(rstd[:], PR2[:], AF.Ln, bias=eps5[:, 0:1], scale=1.0 / 384.0)
                    c_act.inc(e.activation(rstd[:], rstd[:], AF.Exp, scale=-0.5))

            @b.gpsimd
            def _(e):
                if trunc < 7:
                    return
                for s in range(nstr):
                    e.wait_ge(c_dve.sem, dvem((s + 1) * nchk - 1, D_ON))
                    for m in range(3):
                        c_st.inc(e.dma_start(out=on_o[s, m * 128:(m + 1) * 128, :],
                                             in_=Ob[:, (s * 3 + m) * 4096:(s * 3 + m + 1) * 4096]), 16)
                e.wait_ge(c_st.sem, c_st.n)
        return nc


def host_delta_inputs(pre_outs):
    maps = []
    maskui = np.triu(np.ones((128, 128), np.float32))
    maskls = np.tril(np.ones((128, 128), np.float32), -1)
    identb = np.eye(128, dtype=BF)
    identf = np.eye(128, dtype=np.float32)
    onesrow = np.ones((1, 128), np.float32)
    onesmb = np.ones((128, 128), BF)
    onescol = np.ones((128, 1), np.float32)
    for c in range(NCORE):
        bb = c // 2
        heads = (0, 1) if c % 2 == 0 else (2, 3)
        qn = pre_outs["qn"][bb]; kn = pre_outs["kn"][bb]; knt = pre_outs["knt"][bb]
        vtt = pre_outs["vt"][bb]; gkA = pre_outs["gk"][bb]; betaA = pre_outs["beta"][bb]
        qf = np.stack([qn[h * 192:(h + 1) * 192] for h in heads])
        kf = np.stack([kn[h * 192:(h + 1) * 192] for h in heads])
        ktm = np.stack([np.ascontiguousarray(knt[:, h * 192:(h + 1) * 192]) for h in heads])
        vtm = np.stack([np.ascontiguousarray(vtt[:, h * 384:(h + 1) * 384]) for h in heads])
        gkr = np.stack([gkA[h].reshape(32, 128) for h in heads]).astype(np.float32)
        betar = np.stack([betaA[h].reshape(32, 128) for h in heads]).astype(np.float32)
        maps.append(dict(qf=np.ascontiguousarray(qf), kf=np.ascontiguousarray(kf),
                         kt=ktm, vt=vtm, gkr=gkr, betar=betar,
                         maskui=maskui, maskls=maskls, identb=identb, identf=identf,
                         onesrow=onesrow, onesmb=onesmb, onescol=onescol))
    return maps


def emit_ln_stats(nc, b, Xb, TW, som, eps5, mrow, rrow, vrow, P1, P2, sqt, c_pe, c_dve, c_act,
                  nch=8, inv=1.0 / 1024.0):
    spans = []
    o = 0
    while o + 512 <= TW:
        spans.append((o, o + 512))
        o += 512
    if o < TW:
        spans.append((TW - 512, TW))
    nsp = len(spans)
    pe0, dve0, act0 = c_pe.n, c_dve.n, c_act.n

    @b.scalar
    def _(e):
        for si, (lo, hi) in enumerate(spans):
            for f in range(nch):
                kk = si * nch + f
                if kk >= 2:
                    e.wait_ge(c_pe.sem, pe0 + kk - 1)
                c_act.inc(e.activation(sqt[:, (kk % 2) * 512:(kk % 2) * 512 + 512],
                                       Xb[:, f * TW + lo:f * TW + hi], AF.Square))
        for si, (lo, hi) in enumerate(spans):
            e.wait_ge(c_dve.sem, dve0 + si + 1)
            e.activation(rrow[:, lo:hi], vrow[:, lo:hi], AF.Ln, bias=eps5[:, 0:1], scale=1.0)
            c_act.inc(e.activation(rrow[:, lo:hi], rrow[:, lo:hi], AF.Exp, scale=-0.5))

    @b.tensor
    def _(e):
        for si, (lo, hi) in enumerate(spans):
            if si >= 2:
                e.wait_ge(c_dve.sem, dve0 + si - 1)
            for f in range(nch):
                e.matmul(P1[si % 2][:, 0:512], som[:], Xb[:, f * TW + lo:f * TW + hi],
                         start=(f == 0), stop=(f == nch - 1))
            for f in range(nch):
                kk = si * nch + f
                e.wait_ge(c_act.sem, act0 + kk + 1)
                c_pe.inc(e.matmul(P2[si % 2][:, 0:512], som[:],
                                  sqt[:, (kk % 2) * 512:(kk % 2) * 512 + 512],
                                  start=(f == 0), stop=(f == nch - 1)))

    @b.vector
    def _(e):
        for si, (lo, hi) in enumerate(spans):
            e.wait_ge(c_pe.sem, pe0 + (si + 1) * nch)
            e.tensor_scalar(mrow[:, lo:hi], P1[si % 2][:, 0:512], inv, None, ALU.mult)
            e.tensor_scalar(vrow[:, lo:hi], P2[si % 2][:, 0:512], inv, None, ALU.mult)
            e.tensor_tensor(P1[si % 2][:, 0:512], mrow[:, lo:hi], mrow[:, lo:hi], ALU.mult)
            c_dve.inc(e.tensor_tensor(vrow[:, lo:hi], vrow[:, lo:hi], P1[si % 2][:, 0:512], ALU.subtract))
    return spans


def build_post():
    nc = new_nc()
    on_i = nc.dram_tensor("on_i", [1536, TV], BF16, kind="ExternalInput").ap()
    g_i = nc.dram_tensor("g_i", [1536, TV], BF16, kind="ExternalInput").ap()
    zs_i = nc.dram_tensor("zs_i", [D, TV], BF16, kind="ExternalInput").ap()
    x0_i = nc.dram_tensor("x0_i", [D, TV], F32, kind="ExternalInput").ap()
    gnwt = nc.dram_tensor("gnwt", [128, 12], F32, kind="ExternalInput").ap()
    wso = nc.dram_tensor("wso", [1536, D], F32, kind="ExternalInput").ap()
    wout = nc.dram_tensor("wout", [2048, D], F32, kind="ExternalInput").ap()
    onesm = nc.dram_tensor("onesm", [128, 128], BF16, kind="ExternalInput").ap()
    x1_o = nc.dram_tensor("x1_o", [D, TV], F32, kind="ExternalOutput").ap()

    es = ExitStack()
    sb = lambda name, shape, dt: es.enter_context(nc.sbuf_tensor(name, shape, dt))
    ps = lambda name, shape: es.enter_context(nc.psum_tensor(name, shape, F32))

    On = sb("On", [128, 12 * TV], BF16)
    Gg = sb("Gg", [128, 12 * TV], BF16)
    Zs = sb("Zs", [128, NCH * TV], BF16)
    Wbuf = sb("Wbuf", [128, 16 * D], BF16)
    Gn = sb("Gn", [128, 12], F32)
    wstg = sb("wstg", [128, D], F32)
    som = sb("som", [128, 128], BF16)
    eps5 = sb("eps5", [128, 1], F32)
    mrow = sb("mrow", [128, TV], F32)
    rrow = sb("rrow", [128, TV], F32)
    vrow = sb("vrow", [128, TV], F32)
    sqt = sb("sqt", [128, 2 * 512], BF16)
    xtmp = sb("xtmp", [128, 2 * 512], F32)
    x0t = sb("x0t", [128, 2 * 512], F32)
    Tmid = On
    Y2 = Gg

    PQ = [ps(f"PQ{i}", [128, 512]) for i in range(4)]
    P1 = [ps("P1a", [128, 512]), ps("P1b", [128, 512])]
    P2 = [ps("P2a", [128, 512]), ps("P2b", [128, 512])]

    c_ldc = Ctr(es.enter_context(nc.semaphore("s_ldc")))
    c_ldw = Ctr(es.enter_context(nc.semaphore("s_ldw")))
    c_cast = Ctr(es.enter_context(nc.semaphore("s_cast")))
    c_pe = Ctr(es.enter_context(nc.semaphore("s_pe")))
    c_dve = Ctr(es.enter_context(nc.semaphore("s_dve")))
    c_act = Ctr(es.enter_context(nc.semaphore("s_act")))
    c_st = Ctr(es.enter_context(nc.semaphore("s_st")))

    with es:
        with nc.Block() as b:
            @b.sync
            def _(e):
                for f in range(12):
                    c_ldc.inc(e.dma_start(out=On[:, f * TV:(f + 1) * TV], in_=on_i[128 * f:128 * (f + 1), :]), 16)
                    c_ldc.inc(e.dma_start(out=Gg[:, f * TV:(f + 1) * TV], in_=g_i[128 * f:128 * (f + 1), :]), 16)
                for f in range(NCH):
                    c_ldc.inc(e.dma_start(out=Zs[:, f * TV:(f + 1) * TV], in_=zs_i[128 * f:128 * (f + 1), :]), 16)
                c_ldc.inc(e.dma_start(out=Gn[:], in_=gnwt), 16)
                c_ldc.inc(e.dma_start(out=som[:], in_=onesm), 16)
                k = 0
                for f in range(12):
                    k += 1
                    if k >= 2:
                        e.wait_ge(c_cast.sem, k - 1)
                    c_ldw.inc(e.dma_start(out=wstg[:], in_=wso[128 * f:128 * (f + 1), :]), 16)

            @b.gpsimd
            def _(e):
                e.memset(eps5[:], 1e-5)

            @b.vector
            def _(e):
                e.wait_ge(c_ldc.sem, 16 * (24 + 8 + 2))
                k = 0
                for f in range(12):
                    k += 1
                    e.wait_ge(c_ldw.sem, 16 * k)
                    c_cast.inc(e.tensor_scalar(Wbuf[:, f * D:(f + 1) * D], wstg[:], Gn[:, f:f + 1], None, ALU.mult))

        with nc.Block() as b:
            act0, dve0 = c_act.n, c_dve.n

            @b.scalar
            def _(e):
                for f in range(12):
                    c_act.inc(e.activation(Gg[:, f * TV:(f + 1) * TV], Gg[:, f * TV:(f + 1) * TV], AF.Silu))

            @b.vector
            def _(e):
                for f in range(12):
                    e.wait_ge(c_act.sem, act0 + f + 1)
                    c_dve.inc(e.tensor_mul(Gg[:, f * TV:(f + 1) * TV], On[:, f * TV:(f + 1) * TV],
                                           Gg[:, f * TV:(f + 1) * TV]))

        with nc.Block() as b:
            pe0, dve0 = c_pe.n, c_dve.n

            @b.tensor
            def _(e):
                k = 0
                for m in range(NCH):
                    for si, (lo, hi) in enumerate(PSP):
                        if k >= 4:
                            e.wait_ge(c_dve.sem, dve0 + k - 3)
                        for f in range(12):
                            ins = e.matmul(PQ[k % 4][:, 0:512], Wbuf[:, f * D + m * 128:f * D + m * 128 + 128],
                                           Gg[:, f * TV + lo:f * TV + hi], start=(f == 0), stop=(f == 11))
                        c_pe.inc(ins)
                        k += 1

            @b.vector
            def _(e):
                k = 0
                for m in range(NCH):
                    for si, (lo, hi) in enumerate(PSP):
                        e.wait_ge(c_pe.sem, pe0 + k + 1)
                        c_dve.inc(e.tensor_copy(Tmid[:, m * TV + lo:m * TV + hi], PQ[k % 4][:, 0:512]))
                        k += 1

        with nc.Block() as b:
            ld0, ca0 = c_ldw.n, c_cast.n

            @b.sync
            def _(e):
                for f in range(16):
                    if f >= 1:
                        e.wait_ge(c_cast.sem, ca0 + f)
                    c_ldw.inc(e.dma_start(out=wstg[:], in_=wout[128 * f:128 * (f + 1), :]), 16)

            @b.vector
            def _(e):
                for f in range(16):
                    e.wait_ge(c_ldw.sem, ld0 + 16 * (f + 1))
                    c_cast.inc(e.tensor_copy(Wbuf[:, f * D:(f + 1) * D], wstg[:]))

        with nc.Block() as b:
            pe0, dve0 = c_pe.n, c_dve.n

            @b.tensor
            def _(e):
                k = 0
                for m in range(NCH):
                    for si, (lo, hi) in enumerate(PSP):
                        if k >= 4:
                            e.wait_ge(c_dve.sem, dve0 + k - 3)
                        for f in range(NCH):
                            e.matmul(PQ[k % 4][:, 0:512], Wbuf[:, f * D + m * 128:f * D + m * 128 + 128],
                                     Tmid[:, f * TV + lo:f * TV + hi], start=(f == 0), stop=False)
                        for f in range(NCH):
                            ins = e.matmul(PQ[k % 4][:, 0:512], Wbuf[:, (8 + f) * D + m * 128:(8 + f) * D + m * 128 + 128],
                                           Zs[:, f * TV + lo:f * TV + hi], start=False, stop=(f == NCH - 1))
                        c_pe.inc(ins)
                        k += 1

            @b.vector
            def _(e):
                k = 0
                for m in range(NCH):
                    for si, (lo, hi) in enumerate(PSP):
                        e.wait_ge(c_pe.sem, pe0 + k + 1)
                        c_dve.inc(e.tensor_copy(Y2[:, m * TV + lo:m * TV + hi], PQ[k % 4][:, 0:512]))
                        k += 1

        with nc.Block() as b:
            emit_ln_stats(nc, b, Y2, TV, som, eps5, mrow, rrow, vrow, P1, P2, sqt, c_pe, c_dve, c_act)

        with nc.Block() as b:
            dve0 = c_dve.n
            sl0 = [Ctr(es.enter_context(nc.semaphore("s_sl0"))), Ctr(es.enter_context(nc.semaphore("s_sl1")))]

            slx = [Ctr(es.enter_context(nc.semaphore("s_slx0"))), Ctr(es.enter_context(nc.semaphore("s_slx1")))]

            @b.sync
            def _(e):
                k = 0
                for f in range(NCH):
                    for si, (lo, hi) in enumerate(PSP):
                        if k >= 2:
                            e.wait_ge(c_dve.sem, dve0 + k - 1)
                        slx[k % 2].inc(e.dma_start(out=x0t[:, (k % 2) * 512:(k % 2) * 512 + 512],
                                                   in_=x0_i[128 * f:128 * (f + 1), lo:hi]), 16)
                        k += 1

            @b.vector
            def _(e):
                k = 0
                for f in range(NCH):
                    for si, (lo, hi) in enumerate(PSP):
                        if k >= 2:
                            e.wait_ge(sl0[k % 2].sem, 16 * (k // 2))
                        e.wait_ge(slx[k % 2].sem, 16 * (k // 2 + 1))
                        t = xtmp[:, (k % 2) * 512:(k % 2) * 512 + 512]
                        e.tensor_tensor(t, Y2[:, f * TV + lo:f * TV + hi], mrow[:, lo:hi], ALU.subtract)
                        e.tensor_tensor(t, t, rrow[:, lo:hi], ALU.mult)
                        c_dve.inc(e.tensor_tensor(t, t, x0t[:, (k % 2) * 512:(k % 2) * 512 + 512], ALU.add))
                        k += 1

            @b.gpsimd
            def _(e):
                k = 0
                for f in range(NCH):
                    for si, (lo, hi) in enumerate(PSP):
                        e.wait_ge(c_dve.sem, dve0 + k + 1)
                        sl0[k % 2].inc(e.dma_start(out=x1_o[128 * f:128 * (f + 1), lo:hi],
                                                   in_=xtmp[:, (k % 2) * 512:(k % 2) * 512 + 512]), 16)
                        k += 1
                e.wait_ge(sl0[0].sem, sl0[0].n)
                e.wait_ge(sl0[1].sem, sl0[1].n)
    return nc


TH = TV + 1


def build_ffn(stop=None):
    nc = new_nc()
    xh = nc.dram_tensor("xh", [D, TH], F32, kind="ExternalInput").ap()
    wr = nc.dram_tensor("wr", [D, D], F32, kind="ExternalInput").ap()
    wk = nc.dram_tensor("wk", [D, D], F32, kind="ExternalInput").ap()
    wv = nc.dram_tensor("wv", [D, D], F32, kind="ExternalInput").ap()
    mur = nc.dram_tensor("mur", [128, 8], F32, kind="ExternalInput").ap()
    muk = nc.dram_tensor("muk", [128, 8], F32, kind="ExternalInput").ap()
    onesm = nc.dram_tensor("onesm", [128, 128], BF16, kind="ExternalInput").ap()
    xn_o = nc.dram_tensor("xn_o", [D, TV], F32, kind="ExternalOutput").ap()

    es = ExitStack()
    sb = lambda name, shape, dt: es.enter_context(nc.sbuf_tensor(name, shape, dt))
    ps = lambda name, shape: es.enter_context(nc.psum_tensor(name, shape, F32))

    Xb = sb("Xb", [128, NCH * TH], BF16)
    Lnb = sb("Lnb", [128, NCH * TH], BF16)
    Xk = sb("Xk", [128, NCH * TV], BF16)
    Wr = sb("Wr", [128, NCH * D], BF16)
    Wk = sb("Wk", [128, NCH * D], BF16)
    Wv = sb("Wv", [128, NCH * D], BF16)
    Mur = sb("Mur", [128, 8], F32)
    Muk = sb("Muk", [128, 8], F32)
    stg = sb("stg", [128, TH], F32)
    som = sb("som", [128, 128], BF16)
    eps5 = sb("eps5", [128, 1], F32)
    mrow = sb("mrow", [128, TH], F32)
    rrow = sb("rrow", [128, TH], F32)
    vrow = sb("vrow", [128, TH], F32)
    sqt = sb("sqt", [128, 2 * 512], BF16)
    xtmp = sb("xtmp", [128, 2 * 512], F32)
    x0t = sb("x0t", [128, 2 * 512], F32)

    PQ = [ps(f"PQ{i}", [128, 512]) for i in range(4)]
    P1 = [ps("P1a", [128, 512]), ps("P1b", [128, 512])]
    P2 = [ps("P2a", [128, 512]), ps("P2b", [128, 512])]

    c_ldc = Ctr(es.enter_context(nc.semaphore("s_ldc")))
    c_ldw = Ctr(es.enter_context(nc.semaphore("s_ldw")))
    c_cast = Ctr(es.enter_context(nc.semaphore("s_cast")))
    c_pe = Ctr(es.enter_context(nc.semaphore("s_pe")))
    c_dve = Ctr(es.enter_context(nc.semaphore("s_dve")))
    c_act = Ctr(es.enter_context(nc.semaphore("s_act")))
    sl0 = [Ctr(es.enter_context(nc.semaphore("s_sl0"))), Ctr(es.enter_context(nc.semaphore("s_sl1")))]
    slx = [Ctr(es.enter_context(nc.semaphore("s_slx0"))), Ctr(es.enter_context(nc.semaphore("s_slx1")))]

    with es:
        with nc.Block() as b:
            @b.sync
            def _(e):
                c_ldc.inc(e.dma_start(out=som[:], in_=onesm), 16)
                c_ldc.inc(e.dma_start(out=Mur[:], in_=mur), 16)
                c_ldc.inc(e.dma_start(out=Muk[:], in_=muk), 16)
                k = 0
                for f in range(NCH):
                    k += 1
                    if k >= 2:
                        e.wait_ge(c_cast.sem, k - 1)
                    c_ldw.inc(e.dma_start(out=stg[:], in_=xh[128 * f:128 * (f + 1), :]), 16)
                for W, src in ((Wr, wr), (Wk, wk), (Wv, wv)):
                    for f in range(NCH):
                        k += 1
                        e.wait_ge(c_cast.sem, k - 1)
                        c_ldw.inc(e.dma_start(out=stg[:, 0:D], in_=src[128 * f:128 * (f + 1), :]), 16)

            @b.gpsimd
            def _(e):
                e.memset(eps5[:], 1e-5)

            @b.vector
            def _(e):
                e.wait_ge(c_ldc.sem, 16 * 3)
                k = 0
                for f in range(NCH):
                    k += 1
                    e.wait_ge(c_ldw.sem, 16 * k)
                    c_cast.inc(e.tensor_copy(Xb[:, f * TH:(f + 1) * TH], stg[:]))
                for W in (Wr, Wk, Wv):
                    for f in range(NCH):
                        k += 1
                        e.wait_ge(c_ldw.sem, 16 * k)
                        c_cast.inc(e.tensor_copy(W[:, f * D:(f + 1) * D], stg[:, 0:D]))

        with nc.Block() as b:
            emit_ln_stats(nc, b, Xb, TH, som, eps5, mrow, rrow, vrow, P1, P2, sqt, c_pe, c_dve, c_act)

        with nc.Block() as b:
            @b.vector
            def _(e):
                for f in range(NCH):
                    e.tensor_tensor(Lnb[:, f * TH:(f + 1) * TH], Xb[:, f * TH:(f + 1) * TH], mrow[:], ALU.subtract)
                    e.tensor_tensor(Lnb[:, f * TH:(f + 1) * TH], Lnb[:, f * TH:(f + 1) * TH], rrow[:], ALU.mult)
                for f in range(NCH):
                    ln1 = Lnb[:, f * TH + 1:f * TH + 1 + TV]
                    ln0 = Lnb[:, f * TH:f * TH + TV]
                    d = xtmp[:, 0:512]
                    for lo, hi in PSP:
                        dd = xtmp[:, 0:512]
                        e.tensor_tensor(dd, Lnb[:, f * TH + 1 + lo:f * TH + 1 + hi],
                                        Lnb[:, f * TH + lo:f * TH + hi], ALU.subtract)
                        e.scalar_tensor_tensor(Xk[:, f * TV + lo:f * TV + hi], dd, Muk[:, f:f + 1],
                                               Lnb[:, f * TH + lo:f * TH + hi], ALU.mult, ALU.add)
                        e.scalar_tensor_tensor(Xb[:, f * TH + lo:f * TH + hi], dd, Mur[:, f:f + 1],
                                               Lnb[:, f * TH + lo:f * TH + hi], ALU.mult, ALU.add)

        if stop == "mix":
            es.pop_all()
            return nc
        with nc.Block() as b:
            pe0, act0 = c_pe.n, c_act.n

            @b.tensor
            def _(e):
                k = 0
                for m in range(NCH):
                    for si, (lo, hi) in enumerate(PSP):
                        if k >= 4:
                            e.wait_ge(c_act.sem, act0 + k - 3)
                        for f in range(NCH):
                            ins = e.matmul(PQ[k % 4][:, 0:512], Wr[:, f * D + m * 128:f * D + m * 128 + 128],
                                           Xb[:, f * TH + lo:f * TH + hi], start=(f == 0), stop=(f == NCH - 1))
                        c_pe.inc(ins)
                        k += 1

            @b.scalar
            def _(e):
                k = 0
                for m in range(NCH):
                    for si, (lo, hi) in enumerate(PSP):
                        e.wait_ge(c_pe.sem, pe0 + k + 1)
                        c_act.inc(e.activation(Lnb[:, m * TV + lo:m * TV + hi], PQ[k % 4][:, 0:512], AF.Sigmoid))
                        k += 1

        with nc.Block() as b:
            pe0, act0 = c_pe.n, c_act.n

            @b.tensor
            def _(e):
                k = 0
                for m in range(NCH):
                    for si, (lo, hi) in enumerate(PSP):
                        if k >= 4:
                            e.wait_ge(c_act.sem, act0 + k - 3)
                        for f in range(NCH):
                            ins = e.matmul(PQ[k % 4][:, 0:512], Wk[:, f * D + m * 128:f * D + m * 128 + 128],
                                           Xk[:, f * TV + lo:f * TV + hi], start=(f == 0), stop=(f == NCH - 1))
                        c_pe.inc(ins)
                        k += 1

            @b.scalar
            def _(e):
                k = 0
                for m in range(NCH):
                    for si, (lo, hi) in enumerate(PSP):
                        e.wait_ge(c_pe.sem, pe0 + k + 1)
                        e.activation(Xb[:, m * TV + lo:m * TV + hi], PQ[k % 4][:, 0:512], AF.Relu)
                        c_act.inc(e.activation(Xb[:, m * TV + lo:m * TV + hi], Xb[:, m * TV + lo:m * TV + hi], AF.Square))
                        k += 1

        with nc.Block() as b:
            pe0, dve0 = c_pe.n, c_dve.n

            @b.tensor
            def _(e):
                k = 0
                for m in range(NCH):
                    for si, (lo, hi) in enumerate(PSP):
                        if k >= 4:
                            e.wait_ge(c_dve.sem, dve0 + k - 3)
                        for f in range(NCH):
                            ins = e.matmul(PQ[k % 4][:, 0:512], Wv[:, f * D + m * 128:f * D + m * 128 + 128],
                                           Xb[:, f * TV + lo:f * TV + hi], start=(f == 0), stop=(f == NCH - 1))
                        c_pe.inc(ins)
                        k += 1

            @b.vector
            def _(e):
                k = 0
                for m in range(NCH):
                    for si, (lo, hi) in enumerate(PSP):
                        e.wait_ge(c_pe.sem, pe0 + k + 1)
                        c_dve.inc(e.tensor_mul(Xk[:, m * TV + lo:m * TV + hi], PQ[k % 4][:, 0:512],
                                               Lnb[:, m * TV + lo:m * TV + hi]))
                        k += 1

        with nc.Block() as b:
            emit_ln_stats(nc, b, Xk, TV, som, eps5, mrow, rrow, vrow, P1, P2, sqt, c_pe, c_dve, c_act)

        with nc.Block() as b:
            dve0 = c_dve.n

            @b.sync
            def _(e):
                k = 0
                for f in range(NCH):
                    for si, (lo, hi) in enumerate(PSP):
                        if k >= 2:
                            e.wait_ge(c_dve.sem, dve0 + k - 1)
                        slx[k % 2].inc(e.dma_start(out=x0t[:, (k % 2) * 512:(k % 2) * 512 + 512],
                                                   in_=xh[128 * f:128 * (f + 1), 1 + lo:1 + hi]), 16)
                        k += 1

            @b.vector
            def _(e):
                k = 0
                for f in range(NCH):
                    for si, (lo, hi) in enumerate(PSP):
                        if k >= 2:
                            e.wait_ge(sl0[k % 2].sem, 16 * (k // 2))
                        e.wait_ge(slx[k % 2].sem, 16 * (k // 2 + 1))
                        t = xtmp[:, (k % 2) * 512:(k % 2) * 512 + 512]
                        e.tensor_tensor(t, Xk[:, f * TV + lo:f * TV + hi], mrow[:, lo:hi], ALU.subtract)
                        e.tensor_tensor(t, t, rrow[:, lo:hi], ALU.mult)
                        c_dve.inc(e.tensor_tensor(t, t, x0t[:, (k % 2) * 512:(k % 2) * 512 + 512], ALU.add))
                        k += 1

            @b.gpsimd
            def _(e):
                k = 0
                for f in range(NCH):
                    for si, (lo, hi) in enumerate(PSP):
                        e.wait_ge(c_dve.sem, dve0 + k + 1)
                        sl0[k % 2].inc(e.dma_start(out=xn_o[128 * f:128 * (f + 1), lo:hi],
                                                   in_=xtmp[:, (k % 2) * 512:(k % 2) * 512 + 512]), 16)
                        k += 1
                e.wait_ge(sl0[0].sem, sl0[0].n)
                e.wait_ge(sl0[1].sem, sl0[1].n)
    return nc


TA = 4096
SP8 = [(i * 512, (i + 1) * 512) for i in range(8)]


def build_att1():
    nc = new_nc()
    x2f = nc.dram_tensor("x2f", [D, TA], F32, kind="ExternalInput").ap()
    wkT = nc.dram_tensor("wkT", [D, 512], F32, kind="ExternalInput").ap()
    wvT = nc.dram_tensor("wvT", [D, 512], F32, kind="ExternalInput").ap()
    wrT = nc.dram_tensor("wrT", [D, 512], F32, kind="ExternalInput").ap()
    woT = nc.dram_tensor("woT", [512, D], F32, kind="ExternalInput").ap()
    tmix = nc.dram_tensor("tmix", [128, 8], F32, kind="ExternalInput").ap()
    tdh = nc.dram_tensor("tdh", [128, 4], F32, kind="ExternalInput").ap()
    tfh = nc.dram_tensor("tfh", [128, 4], F32, kind="ExternalInput").ap()
    onesm = nc.dram_tensor("onesm", [128, 128], BF16, kind="ExternalInput").ap()
    onesf = nc.dram_tensor("onesf", [128, 128], F32, kind="ExternalInput").ap()
    yp_o = nc.dram_tensor("yp_o", [D, TA], F32, kind="ExternalOutput").ap()

    es = ExitStack()
    sb = lambda name, shape, dt: es.enter_context(nc.sbuf_tensor(name, shape, dt))
    ps = lambda name, shape: es.enter_context(nc.psum_tensor(name, shape, F32))

    Wk = sb("Wk", [128, NCH * 512], BF16)
    Wv = sb("Wv", [128, NCH * 512], BF16)
    Wr = sb("Wr", [128, NCH * 512], BF16)
    Wo = sb("Wo", [128, 4 * D], BF16)
    Tm = sb("Tm", [128, 8], F32)
    Dec = sb("Dec", [128, 4], F32)
    Tfx = sb("Tfx", [128, 4], F32)
    stg = sb("stg", [128, NCH * 513], F32)
    mrow = sb("mrow", [128, 1 + TA], BF16)
    rrow = sb("rrow", [128, 1 + TA], BF16)
    vrow = sb("vrow", [128, 2 * 512], F32)
    lnx = sb("lnx", [128, 513], BF16)
    xms = sb("xms", [128, NCH * 512], BF16)
    kbf = sb("kbf", [128, TA], BF16)
    vbf = sb("vbf", [128, TA], BF16)
    kvb = sb("kvb", [128, TA], BF16)
    rsc = sb("rsc", [128, TA], BF16)
    fkv = sb("fkv", [128, TA], BF16)
    fk = sb("fk", [128, TA], F32)
    rsig = sb("rsig", [128, 4 * TA], BF16)
    rwkv = sb("rwkv", [128, 4 * TA], BF16)
    sqt = sb("sqt", [128, 2 * 512], BF16)
    som = sb("som", [128, 128], BF16)
    somf = sb("somf", [128, 128], F32)
    eps5 = sb("eps5", [128, 1], F32)
    otmp = sb("otmp", [128, 512], F32)
    wstg = sb("wstg", [128, D], F32)

    PK = [ps(f"PK{i}", [128, 512]) for i in range(2)]
    PV = [ps(f"PV{i}", [128, 512]) for i in range(2)]
    P1 = [ps("P1a", [128, 512]), ps("P1b", [128, 512])]
    P2 = [ps("P2a", [128, 512]), ps("P2b", [128, 512])]

    c_ldc = Ctr(es.enter_context(nc.semaphore("s_ldc")))
    c_ldw = Ctr(es.enter_context(nc.semaphore("s_ldw")))
    c_lds = Ctr(es.enter_context(nc.semaphore("s_lds")))
    c_cast = Ctr(es.enter_context(nc.semaphore("s_cast")))
    c_pe = Ctr(es.enter_context(nc.semaphore("s_pe")))
    c_dve = Ctr(es.enter_context(nc.semaphore("s_dve")))
    c_act = Ctr(es.enter_context(nc.semaphore("s_act")))
    sl0 = [Ctr(es.enter_context(nc.semaphore("s_sl0"))), Ctr(es.enter_context(nc.semaphore("s_sl1")))]
    c_z = Ctr(es.enter_context(nc.semaphore("s_z")))

    def emit_span_loads(e, dve_mark_fn):
        for si, (lo, hi) in enumerate(SP8):
            if si >= 1:
                e.wait_ge(c_dve.sem, dve_mark_fn(si - 1))
            for f in range(NCH):
                if lo == 0:
                    c_lds.inc(e.dma_start(out=stg[:, f * 513 + 1:f * 513 + 513],
                                          in_=x2f[128 * f:128 * (f + 1), 0:512]), 16)
                else:
                    c_lds.inc(e.dma_start(out=stg[:, f * 513:f * 513 + 513],
                                          in_=x2f[128 * f:128 * (f + 1), lo - 1:hi]), 16)

    def emit_xm_build(e, si, lo, hi):
        for f in range(NCH):
            sl2 = 0
            e.tensor_tensor(lnx[:, sl2:sl2 + 513], stg[:, f * 513:f * 513 + 513],
                            mrow[:, lo:hi + 1], ALU.subtract)
            e.tensor_tensor(lnx[:, sl2:sl2 + 513], lnx[:, sl2:sl2 + 513],
                            rrow[:, lo:hi + 1], ALU.mult)
            e.tensor_tensor(otmp[:, 0:512], lnx[:, sl2 + 1:sl2 + 513],
                            lnx[:, sl2:sl2 + 512], ALU.subtract)
            e.scalar_tensor_tensor(xms[:, f * 512:(f + 1) * 512], otmp[:, 0:512], Tm[:, f:f + 1],
                                   lnx[:, sl2:sl2 + 512], ALU.mult, ALU.add)

    with es:
        with nc.Block() as b:
            @b.sync
            def _(e):
                c_ldc.inc(e.dma_start(out=som[:], in_=onesm), 16)
                c_ldc.inc(e.dma_start(out=somf[:], in_=onesf), 16)
                c_ldc.inc(e.dma_start(out=Tm[:], in_=tmix), 16)
                c_ldc.inc(e.dma_start(out=Dec[:], in_=tdh), 16)
                c_ldc.inc(e.dma_start(out=Tfx[:], in_=tfh), 16)
                k = 0
                for W, src, wd in ((Wk, wkT, 512), (Wv, wvT, 512), (Wr, wrT, 512)):
                    for f in range(NCH):
                        k += 1
                        if k >= 2:
                            e.wait_ge(c_cast.sem, k - 1)
                        c_ldw.inc(e.dma_start(out=wstg[:, 0:wd], in_=src[128 * f:128 * (f + 1), :]), 16)
                for f in range(4):
                    k += 1
                    e.wait_ge(c_cast.sem, k - 1)
                    c_ldw.inc(e.dma_start(out=wstg[:, 0:D], in_=woT[128 * f:128 * (f + 1), :]), 16)

            @b.gpsimd
            def _(e):
                e.memset(eps5[:], 1e-5)
                e.memset(mrow[:, 0:1], 0.0)
                e.memset(rrow[:, 0:1], 0.0)
                for f in range(NCH):
                    e.memset(stg[:, f * 513:f * 513 + 1], 0.0)

            @b.vector
            def _(e):
                e.wait_ge(c_ldc.sem, 16 * 5)
                k = 0
                for W, src, wd in ((Wk, wkT, 512), (Wv, wvT, 512), (Wr, wrT, 512)):
                    for f in range(NCH):
                        k += 1
                        e.wait_ge(c_ldw.sem, 16 * k)
                        c_cast.inc(e.tensor_copy(W[:, f * wd:(f + 1) * wd], wstg[:, 0:wd]))
                for f in range(4):
                    k += 1
                    e.wait_ge(c_ldw.sem, 16 * k)
                    c_cast.inc(e.tensor_copy(Wo[:, f * D:(f + 1) * D], wstg[:, 0:D]))

            @b.scalar
            def _(e):
                e.wait_ge(c_ldc.sem, 16 * 5)
                e.activation(Dec[:], Dec[:], AF.Exp)
                e.activation(Dec[:], Dec[:], AF.Exp, scale=-1.0)
                e.activation(Tfx[:], Tfx[:], AF.Exp)

        with nc.Block() as b:
            pe0, dve0, act0 = c_pe.n, c_dve.n, c_act.n
            lds0 = c_lds.n

            @b.sync
            def _(e):
                for si, (lo, hi) in enumerate(SP8):
                    if si >= 1:
                        e.wait_ge(c_pe.sem, pe0 + si * 8)
                    for f in range(NCH):
                        c_lds.inc(e.dma_start(out=stg[:, f * 513 + 1:f * 513 + 513],
                                              in_=x2f[128 * f:128 * (f + 1), lo:hi]), 16)

            sq_mark = {}
            rstd_mark = {}
            _a = act0
            for si in range(8):
                _a += 8
                sq_mark[si] = _a
                if si >= 2:
                    _a += 1
                    rstd_mark[si - 2] = _a
            for si in (6, 7):
                _a += 1
                rstd_mark[si] = _a

            @b.scalar
            def _(e):
                def lnexp(sj):
                    lo2, hi2 = SP8[sj]
                    e.wait_ge(c_dve.sem, dve0 + sj + 1)
                    e.activation(rrow[:, 1 + lo2:1 + hi2], vrow[:, (sj % 2) * 512:(sj % 2) * 512 + 512],
                                 AF.Ln, bias=eps5[:, 0:1], scale=1.0)
                    c_act.inc(e.activation(rrow[:, 1 + lo2:1 + hi2], rrow[:, 1 + lo2:1 + hi2], AF.Exp, scale=-0.5))
                for si, (lo, hi) in enumerate(SP8):
                    e.wait_ge(c_lds.sem, lds0 + 16 * 8 * (si + 1))
                    for f in range(NCH):
                        kk = si * 8 + f
                        if kk >= 2:
                            e.wait_ge(c_pe.sem, pe0 + kk - 1)
                        c_act.inc(e.activation(sqt[:, (kk % 2) * 512:(kk % 2) * 512 + 512],
                                               stg[:, f * 513 + 1:f * 513 + 513], AF.Square))
                    if si >= 2:
                        lnexp(si - 2)
                for sj in (6, 7):
                    lnexp(sj)

            @b.tensor
            def _(e):
                for si, (lo, hi) in enumerate(SP8):
                    e.wait_ge(c_lds.sem, lds0 + 16 * 8 * (si + 1))
                    if si >= 2:
                        e.wait_ge(c_dve.sem, dve0 + si - 1)
                    for f in range(NCH):
                        e.matmul(P1[si % 2][:, 0:512], somf[:], stg[:, f * 513 + 1:f * 513 + 513],
                                 start=(f == 0), stop=(f == NCH - 1))
                    for f in range(NCH):
                        kk = si * 8 + f
                        e.wait_ge(c_act.sem, act0 + kk + 1 + max(0, si - 2))
                        c_pe.inc(e.matmul(P2[si % 2][:, 0:512], som[:],
                                          sqt[:, (kk % 2) * 512:(kk % 2) * 512 + 512],
                                          start=(f == 0), stop=(f == NCH - 1)))

            @b.vector
            def _(e):
                for si, (lo, hi) in enumerate(SP8):
                    vs = vrow[:, (si % 2) * 512:(si % 2) * 512 + 512]
                    e.wait_ge(c_pe.sem, pe0 + (si + 1) * 8)
                    if si >= 2:
                        e.wait_ge(c_act.sem, rstd_mark[si - 2])
                    e.tensor_scalar(mrow[:, 1 + lo:1 + hi], P1[si % 2][:, 0:512], 1.0 / 1024.0, None, ALU.mult)
                    e.tensor_scalar(vs, P2[si % 2][:, 0:512], 1.0 / 1024.0, None, ALU.mult)
                    e.tensor_tensor(P1[si % 2][:, 0:512], mrow[:, 1 + lo:1 + hi], mrow[:, 1 + lo:1 + hi], ALU.mult)
                    c_dve.inc(e.tensor_tensor(vs, vs, P1[si % 2][:, 0:512], ALU.subtract))

        with nc.Block() as b:
            pe0, dve0, act0 = c_pe.n, c_dve.n, c_act.n
            z0 = c_z.n

            @b.gpsimd
            def _(e):
                for f in range(NCH):
                    ins = e.memset(stg[:, f * 513:f * 513 + 1], 0.0)
                c_z.inc(ins)

            @b.sync
            def _(e):
                emit_span_loads(e, lambda s: dve0 + (s + 1))

            @b.vector
            def _(e):
                e.wait_ge(c_z.sem, z0 + 1)
                for si, (lo, hi) in enumerate(SP8):
                    e.wait_ge(c_lds.sem, c_lds.n + 16 * 8 * (si + 1 - 8))
                    if si >= 2:
                        e.wait_ge(c_pe.sem, pe0 + (si - 1) * 4)
                    emit_xm_build(e, si, lo, hi)
                    c_dve.inc(e.tensor_copy(sqt[:, 0:1], sqt[:, 0:1]))

            @b.tensor
            def _(e):
                for si, (lo, hi) in enumerate(SP8):
                    e.wait_ge(c_dve.sem, dve0 + si + 1)
                    for m in range(4):
                        if si * 4 + m >= 2:
                            e.wait_ge(c_act.sem, act0 + si * 4 + m - 1)
                        for f in range(NCH):
                            ins = e.matmul(PK[(si * 4 + m) % 2][:, 0:512],
                                           Wr[:, f * 512 + m * 128:f * 512 + m * 128 + 128],
                                           xms[:, f * 512:(f + 1) * 512], start=(f == 0), stop=(f == NCH - 1))
                        c_pe.inc(ins)

            @b.scalar
            def _(e):
                for si, (lo, hi) in enumerate(SP8):
                    for m in range(4):
                        e.wait_ge(c_pe.sem, pe0 + si * 4 + m + 1)
                        c_act.inc(e.activation(rsig[:, m * TA + lo:m * TA + hi],
                                               PK[(si * 4 + m) % 2][:, 0:512], AF.Sigmoid))

        for m in range(4):
            with nc.Block() as b:
                pe0, dve0, act0 = c_pe.n, c_dve.n, c_act.n
                z0 = c_z.n

                @b.gpsimd
                def _(e, m=m):
                    for f in range(NCH):
                        ins = e.memset(stg[:, f * 513:f * 513 + 1], 0.0)
                    c_z.inc(ins)

                @b.sync
                def _(e, m=m):
                    emit_span_loads(e, lambda s: dve0 + (s + 1) * 3)

                @b.vector
                def _(e, m=m, z0=z0):
                    e.wait_ge(c_z.sem, z0 + 1)
                    for si, (lo, hi) in enumerate(SP8):
                        e.wait_ge(c_lds.sem, c_lds.n + 16 * 8 * (si + 1 - 8))
                        if si >= 2:
                            e.wait_ge(c_pe.sem, pe0 + (si - 1) * 2)
                        emit_xm_build(e, si, lo, hi)
                        c_dve.inc(e.tensor_copy(sqt[:, 0:1], sqt[:, 0:1]))
                        e.wait_ge(c_pe.sem, pe0 + (si + 1) * 2)
                        if si >= 1:
                            e.wait_ge(c_act.sem, act0 + si)
                        e.tensor_scalar(otmp[:], PK[si % 2][:, 0:512], 60.0, None, ALU.min)
                        c_dve.inc(e.tensor_copy(vbf[:, lo:hi], PV[si % 2][:, 0:512]))
                        c_dve.inc(e.tensor_copy(sqt[:, 512:513], sqt[:, 512:513]))

                @b.scalar
                def _(e, m=m):
                    for si, (lo, hi) in enumerate(SP8):
                        e.wait_ge(c_dve.sem, dve0 + si * 3 + 2)
                        c_act.inc(e.activation(kbf[:, lo:hi], otmp[:], AF.Exp))

                @b.tensor
                def _(e, m=m):
                    for si, (lo, hi) in enumerate(SP8):
                        e.wait_ge(c_dve.sem, dve0 + si * 3 + 1)
                        if si >= 2:
                            e.wait_ge(c_dve.sem, dve0 + (si - 2) * 3 + 3)
                        for f in range(NCH):
                            e.matmul(PK[si % 2][:, 0:512], Wk[:, f * 512 + m * 128:f * 512 + m * 128 + 128],
                                     xms[:, f * 512:(f + 1) * 512], start=(f == 0), stop=(f == NCH - 1))
                        for f in range(NCH):
                            ins = e.matmul(PV[si % 2][:, 0:512], Wv[:, f * 512 + m * 128:f * 512 + m * 128 + 128],
                                           xms[:, f * 512:(f + 1) * 512], start=(f == 0), stop=(f == NCH - 1))
                        c_pe.inc(ins, 2)

            with nc.Block() as b:
                dve0, act0 = c_dve.n, c_act.n

                @b.vector
                def _(e, m=m):
                    e.tensor_mul(kvb[:], kbf[:], vbf[:])
                    e.tensor_tensor_scan(rsc[:], Dec[:, m:m + 1].broadcast_to([128, TA]), kvb[:], 0.0, ALU.mult, ALU.add)
                    e.tensor_scalar(fkv[:, 0:1], kvb[:, 0:1], Tfx[:, m:m + 1], None, ALU.mult)
                    e.scalar_tensor_tensor(fkv[:, 1:TA], kvb[:, 1:TA], Tfx[:, m:m + 1], rsc[:, 0:TA - 1], ALU.mult, ALU.add)
                    e.tensor_tensor_scan(rsc[:], Dec[:, m:m + 1].broadcast_to([128, TA]), kbf[:], 0.0, ALU.mult, ALU.add)
                    e.tensor_scalar(fk[:, 0:1], kbf[:, 0:1], Tfx[:, m:m + 1], None, ALU.mult)
                    e.scalar_tensor_tensor(fk[:, 1:TA], kbf[:, 1:TA], Tfx[:, m:m + 1], rsc[:, 0:TA - 1], ALU.mult, ALU.add)
                    e.tensor_scalar(fk[:], fk[:], 1e-9, None, ALU.add)
                    e.reciprocal(fk[:], fk[:])
                    e.tensor_mul(fkv[:], fkv[:], fk[:])
                    c_dve.inc(e.tensor_mul(rwkv[:, m * TA:(m + 1) * TA], fkv[:], rsig[:, m * TA:(m + 1) * TA]))

        with nc.Block() as b:
            pe0, dve0 = c_pe.n, c_dve.n

            @b.tensor
            def _(e):
                k = 0
                for mo in range(NCH):
                    for si, (lo, hi) in enumerate(SP8):
                        if k >= 2:
                            e.wait_ge(c_dve.sem, dve0 + k - 1)
                        for f in range(4):
                            ins = e.matmul(PK[k % 2][:, 0:512], Wo[:, f * D + mo * 128:f * D + mo * 128 + 128],
                                           rwkv[:, f * TA + lo:f * TA + hi], start=(f == 0), stop=(f == 3))
                        c_pe.inc(ins)
                        k += 1

            @b.vector
            def _(e):
                k = 0
                for mo in range(NCH):
                    for si, (lo, hi) in enumerate(SP8):
                        e.wait_ge(c_pe.sem, pe0 + k + 1)
                        if k >= 2:
                            e.wait_ge(sl0[k % 2].sem, 16 * (k // 2))
                        c_dve.inc(e.tensor_copy(vrow[:, (k % 2) * 512:(k % 2) * 512 + 512], PK[k % 2][:, 0:512]))
                        k += 1

            @b.gpsimd
            def _(e):
                k = 0
                for mo in range(NCH):
                    for si, (lo, hi) in enumerate(SP8):
                        e.wait_ge(c_dve.sem, dve0 + k + 1)
                        sl0[k % 2].inc(e.dma_start(out=yp_o[mo * 128:(mo + 1) * 128, lo:hi],
                                                   in_=vrow[:, (k % 2) * 512:(k % 2) * 512 + 512]), 16)
                        k += 1
                e.wait_ge(sl0[0].sem, sl0[0].n)
                e.wait_ge(sl0[1].sem, sl0[1].n)
    return nc


def build_wo3():
    nc = new_nc()
    p0 = nc.dram_tensor("p0", [D, TV], F32, kind="ExternalInput").ap()
    p1 = nc.dram_tensor("p1", [D, TV], F32, kind="ExternalInput").ap()
    x2i = nc.dram_tensor("x2i", [D, TV], F32, kind="ExternalInput").ap()
    onesm = nc.dram_tensor("onesm", [128, 128], BF16, kind="ExternalInput").ap()
    x3_o = nc.dram_tensor("x3_o", [D, TV], F32, kind="ExternalOutput").ap()

    es = ExitStack()
    sb = lambda name, shape, dt: es.enter_context(nc.sbuf_tensor(name, shape, dt))
    ps = lambda name, shape: es.enter_context(nc.psum_tensor(name, shape, F32))

    Y = sb("Y", [128, NCH * TV], BF16)
    stg0 = sb("stg0", [128, TV], F32)
    stg1 = sb("stg1", [128, TV], F32)
    som = sb("som", [128, 128], BF16)
    eps5 = sb("eps5", [128, 1], F32)
    mrow = sb("mrow", [128, TV], F32)
    rrow = sb("rrow", [128, TV], F32)
    vrow = sb("vrow", [128, TV], F32)
    sqt = sb("sqt", [128, 2 * 512], BF16)
    xtmp = sb("xtmp", [128, 2 * 512], F32)
    x0t = sb("x0t", [128, 2 * 512], F32)

    P1 = [ps("P1a", [128, 512]), ps("P1b", [128, 512])]
    P2 = [ps("P2a", [128, 512]), ps("P2b", [128, 512])]

    c_ldc = Ctr(es.enter_context(nc.semaphore("s_ldc")))
    c_l0 = Ctr(es.enter_context(nc.semaphore("s_l0")))
    c_l1 = Ctr(es.enter_context(nc.semaphore("s_l1")))
    c_pe = Ctr(es.enter_context(nc.semaphore("s_pe")))
    c_dve = Ctr(es.enter_context(nc.semaphore("s_dve")))
    c_act = Ctr(es.enter_context(nc.semaphore("s_act")))
    sl0 = [Ctr(es.enter_context(nc.semaphore("s_sl0"))), Ctr(es.enter_context(nc.semaphore("s_sl1")))]
    slx = [Ctr(es.enter_context(nc.semaphore("s_slx0"))), Ctr(es.enter_context(nc.semaphore("s_slx1")))]

    with es:
        with nc.Block() as b:
            @b.sync
            def _(e):
                c_ldc.inc(e.dma_start(out=som[:], in_=onesm), 16)
                for f in range(NCH):
                    if f >= 1:
                        e.wait_ge(c_dve.sem, f)
                    c_l0.inc(e.dma_start(out=stg0[:], in_=p0[128 * f:128 * (f + 1), :]), 16)
                    c_l1.inc(e.dma_start(out=stg1[:], in_=p1[128 * f:128 * (f + 1), :]), 16)

            @b.gpsimd
            def _(e):
                e.memset(eps5[:], 1e-5)

            @b.vector
            def _(e):
                for f in range(NCH):
                    e.wait_ge(c_l0.sem, 16 * (f + 1))
                    e.wait_ge(c_l1.sem, 16 * (f + 1))
                    c_dve.inc(e.tensor_add(Y[:, f * TV:(f + 1) * TV], stg0[:], stg1[:]))

        with nc.Block() as b:
            emit_ln_stats(nc, b, Y, TV, som, eps5, mrow, rrow, vrow, P1, P2, sqt, c_pe, c_dve, c_act)

        with nc.Block() as b:
            dve0 = c_dve.n

            @b.sync
            def _(e):
                k = 0
                for f in range(NCH):
                    for si, (lo, hi) in enumerate(PSP):
                        if k >= 2:
                            e.wait_ge(c_dve.sem, dve0 + k - 1)
                        slx[k % 2].inc(e.dma_start(out=x0t[:, (k % 2) * 512:(k % 2) * 512 + 512],
                                                   in_=x2i[128 * f:128 * (f + 1), lo:hi]), 16)
                        k += 1

            @b.vector
            def _(e):
                k = 0
                for f in range(NCH):
                    for si, (lo, hi) in enumerate(PSP):
                        if k >= 2:
                            e.wait_ge(sl0[k % 2].sem, 16 * (k // 2))
                        e.wait_ge(slx[k % 2].sem, 16 * (k // 2 + 1))
                        t = xtmp[:, (k % 2) * 512:(k % 2) * 512 + 512]
                        e.tensor_tensor(t, Y[:, f * TV + lo:f * TV + hi], mrow[:, lo:hi], ALU.subtract)
                        e.tensor_tensor(t, t, rrow[:, lo:hi], ALU.mult)
                        c_dve.inc(e.tensor_tensor(t, t, x0t[:, (k % 2) * 512:(k % 2) * 512 + 512], ALU.add))
                        k += 1

            @b.gpsimd
            def _(e):
                k = 0
                for f in range(NCH):
                    for si, (lo, hi) in enumerate(PSP):
                        e.wait_ge(c_dve.sem, dve0 + k + 1)
                        sl0[k % 2].inc(e.dma_start(out=x3_o[128 * f:128 * (f + 1), lo:hi],
                                                   in_=xtmp[:, (k % 2) * 512:(k % 2) * 512 + 512]), 16)
                        k += 1
                e.wait_ge(sl0[0].sem, sl0[0].n)
                e.wait_ge(sl0[1].sem, sl0[1].n)
    return nc


def kernel(**ins):
    ins = {k: np.asarray(v) for k, v in ins.items()}
    onesm = np.ones((128, 128), BF)
    onesf = np.ones((128, 128), np.float32)

    maps = host_preA_inputs(ins)
    res = run_bass_kernel_spmd(build_preA(), maps, core_ids=list(range(NCORE))).results
    xs_l = [np.asarray(r["xs_o"]) for r in res]
    zs_l = [np.asarray(r["zs_o"]) for r in res]

    maps = host_preB_inputs(ins, xs_l)
    res = run_bass_kernel_spmd(build_preB(), maps, core_ids=list(range(NCORE))).results

    pre = dict(qn=[], kn=[], knt=[], vt=[], gk=[], beta=[])
    g_l = []
    for b in range(4):
        r0, r1 = res[2 * b], res[2 * b + 1]
        pre["qn"].append(np.concatenate([np.asarray(r0["qn_o"]), np.asarray(r1["qn_o"])], axis=1))
        pre["kn"].append(np.concatenate([np.asarray(r0["kn_o"]), np.asarray(r1["kn_o"])], axis=1))
        knts = []
        for rr, off in ((r0, 0), (r1, TV)):
            raw = np.asarray(rr["knt_o"])
            knt = np.zeros((TV, 768), BF)
            t = 0
            for m in range(6):
                for tt in range(16):
                    knt[tt * 128:(tt + 1) * 128, m * 128:(m + 1) * 128] = raw[t * 128:(t + 1) * 128]
                    t += 1
            knts.append(knt)
        pre["knt"].append(np.concatenate(knts, axis=0))
        pre["vt"].append(np.concatenate([np.asarray(r0["vt_o"]), np.asarray(r1["vt_o"])], axis=0))
        pre["gk"].append(np.concatenate([np.asarray(r0["gk_o"]), np.asarray(r1["gk_o"])], axis=1))
        pre["beta"].append(np.concatenate([np.asarray(r0["beta_o"]), np.asarray(r1["beta_o"])], axis=1))
    g_l = [np.asarray(r["g_o"]) for r in res]

    maps = host_delta_inputs(pre)
    res = run_bass_kernel_spmd(build_delta(), maps, core_ids=list(range(NCORE))).results
    on_b = []
    for b in range(4):
        on = np.zeros((1536, 4096), BF)
        for half in range(2):
            r = np.asarray(res[2 * b + half]["on_o"])
            for si in range(2):
                h = 2 * half + si
                on[h * 384:(h + 1) * 384, :] = r[si]
        on_b.append(on)

    gnwt = np.tile(np.asarray(ins["L0_attn_g_norm_w"], np.float32), 4).reshape(12, 128).T.copy()
    wso = np.ascontiguousarray(ins["L0_attn_W_so"], np.float32)
    wout = np.ascontiguousarray(ins["L0_attn_W_out"], np.float32)
    maps = []
    for c in range(NCORE):
        b, h = c // 2, c % 2
        sl = slice(h * TV, (h + 1) * TV)
        x0 = np.ascontiguousarray(np.asarray(ins["L0_attn_x"][b], np.float32).T[:, sl])
        maps.append(dict(on_i=np.ascontiguousarray(on_b[b][:, sl]), g_i=g_l[c],
                         zs_i=zs_l[c], x0_i=x0, gnwt=gnwt, wso=wso, wout=wout, onesm=onesm))
    res = run_bass_kernel_spmd(build_post(), maps, core_ids=list(range(NCORE))).results
    x1_b = [np.concatenate([np.asarray(res[2 * b]["x1_o"]), np.asarray(res[2 * b + 1]["x1_o"])], axis=1)
            for b in range(4)]

    def ffn_maps(x_b, pref):
        wrT = np.ascontiguousarray(np.asarray(ins[pref + "W_r"], np.float32).T)
        wkT = np.ascontiguousarray(np.asarray(ins[pref + "W_k"], np.float32).T)
        wvT = np.ascontiguousarray(np.asarray(ins[pref + "W_v"], np.float32).T)
        mur = np.ascontiguousarray(np.asarray(ins[pref + "mu_r"], np.float32).reshape(8, 128).T)
        muk = np.ascontiguousarray(np.asarray(ins[pref + "mu_k"], np.float32).reshape(8, 128).T)
        mm = []
        for c in range(NCORE):
            b, h = c // 2, c % 2
            g0 = h * TV - 1
            xh = np.zeros((D, TH), np.float32)
            lo = max(g0, 0)
            xh[:, lo - g0:] = x_b[b][:, lo:g0 + TH]
            mm.append(dict(xh=xh, wr=wrT, wk=wkT, wv=wvT, mur=mur, muk=muk, onesm=onesm))
        return mm

    res = run_bass_kernel_spmd(build_ffn(), ffn_maps(x1_b, "L0_ffn_"), core_ids=list(range(NCORE))).results
    x2_b = [np.concatenate([np.asarray(res[2 * b]["xn_o"]), np.asarray(res[2 * b + 1]["xn_o"])], axis=1)
            for b in range(4)]

    WkT = np.ascontiguousarray(np.asarray(ins["L1_attn_Wk"], np.float32).T)
    WvT = np.ascontiguousarray(np.asarray(ins["L1_attn_Wv"], np.float32).T)
    WrT = np.ascontiguousarray(np.asarray(ins["L1_attn_Wr"], np.float32).T)
    WoT = np.ascontiguousarray(np.asarray(ins["L1_attn_Wo"], np.float32).T)
    tmixh = np.ascontiguousarray(np.asarray(ins["L1_attn_time_mix"], np.float32)[0, 0].reshape(8, 128).T)
    maps = []
    for c in range(NCORE):
        b, ch = c // 2, c % 2
        cs = slice(ch * 512, (ch + 1) * 512)
        tdh = np.ascontiguousarray(np.asarray(ins["L1_attn_time_decay"], np.float32)[cs, 0].reshape(4, 128).T)
        tfh = np.ascontiguousarray(np.asarray(ins["L1_attn_time_first"], np.float32)[cs, 0].reshape(4, 128).T)
        maps.append(dict(x2f=np.ascontiguousarray(x2_b[b]), wkT=np.ascontiguousarray(WkT[:, cs]),
                         wvT=np.ascontiguousarray(WvT[:, cs]), wrT=np.ascontiguousarray(WrT[:, cs]),
                         woT=np.ascontiguousarray(WoT[cs, :]), tmix=tmixh, tdh=tdh, tfh=tfh,
                         onesm=onesm, onesf=onesf))
    res = run_bass_kernel_spmd(build_att1(), maps, core_ids=list(range(NCORE))).results
    yp = [np.asarray(r["yp_o"]) for r in res]

    maps = []
    for c in range(NCORE):
        b, h = c // 2, c % 2
        sl = slice(h * TV, (h + 1) * TV)
        maps.append(dict(p0=np.ascontiguousarray(yp[2 * b][:, sl]), p1=np.ascontiguousarray(yp[2 * b + 1][:, sl]),
                         x2i=np.ascontiguousarray(x2_b[b][:, sl]), onesm=onesm))
    res = run_bass_kernel_spmd(build_wo3(), maps, core_ids=list(range(NCORE))).results
    x3_b = [np.concatenate([np.asarray(res[2 * b]["x3_o"]), np.asarray(res[2 * b + 1]["x3_o"])], axis=1)
            for b in range(4)]

    res = run_bass_kernel_spmd(build_ffn(), ffn_maps(x3_b, "L1_ffn_"), core_ids=list(range(NCORE))).results
    out = np.zeros((4, 4096, 1024), np.float32)
    for b in range(4):
        out[b, 0:TV] = np.asarray(res[2 * b]["xn_o"]).T
        out[b, TV:] = np.asarray(res[2 * b + 1]["xn_o"]).T
    return out
